# version 01 (final)
# baseline (speedup 1.0000x reference)
import contextlib
import numpy as np
import ml_dtypes
import concourse.bass as bass
import concourse.mybir as mybir
from concourse.bass_utils import run_bass_kernel_spmd

F32 = mybir.dt.float32
BF16 = mybir.dt.bfloat16
U8 = mybir.dt.uint8
AF = mybir.ActivationFunctionType
ALU = mybir.AluOpType
AX = mybir.AxisListType

T, C = 2048, 1024
NT = T // 128
NCB = C // 128
LN_EPS = 1e-5
NEG_BIG = -1e30
BF = np.dtype(ml_dtypes.bfloat16)


class Sem:
    def __init__(self, es, nc, name):
        self.h = es.enter_context(nc.semaphore(name))
        self.val = 0

    def inc(self, instr, n=1):
        instr.then_inc(self.h, n)
        self.val += n
        return self.val


class Stage:
    NCONST = 32

    def __init__(self, name):
        self.name = name
        self.nc = bass.Bass("TRN2", target_bir_lowering=False)
        self.es = contextlib.ExitStack()
        self.host_prep = None
        self._consts = {}
        self.CONSTS = None
        self.cvals_dram = self.dram_in("cvals", [128, self.NCONST])

    def cvec(self, value):
        value = float(value)
        if value not in self._consts:
            assert len(self._consts) < self.NCONST
            self._consts[value] = len(self._consts)
        i = self._consts[value]
        return self.CONSTS[:, i:i + 1]

    def alloc_consts(self):
        self.CONSTS = self.sbuf("CONSTS", [128, self.NCONST])

    def load_consts(self, block, sm):
        @block.sync
        def _(sy):
            sm["d"].inc(sy.dma_start(out=self.CONSTS[:, :], in_=self.cvals_dram[:, :]), 16)

    def const_array(self):
        a = np.zeros((128, self.NCONST), np.float32)
        for v, i in self._consts.items():
            a[:, i] = v
        return a

    def dram_in(self, name, shape, dtype=F32):
        return self.nc.dram_tensor(name, list(shape), dtype, kind="ExternalInput").ap()

    def dram_out(self, name, shape, dtype=F32):
        return self.nc.dram_tensor(name, list(shape), dtype, kind="ExternalOutput").ap()

    def sbuf(self, name, shape, dtype=F32):
        return self.es.enter_context(self.nc.sbuf_tensor(name, list(shape), dtype))

    def psum(self, name, shape, dtype=F32):
        return self.es.enter_context(self.nc.psum_tensor(name, list(shape), dtype))

    def sem(self, name):
        return Sem(self.es, self.nc, name)

    def sems(self):
        return {k: self.sem(k) for k in ("d", "g", "t", "v", "s", "p")}


def host_bf16(a):
    return np.ascontiguousarray(a.astype(BF))


def rep128(vec):
    return np.ascontiguousarray(np.broadcast_to(np.asarray(vec, np.float32)[None, :], (128, len(vec))))


IDENT128 = np.eye(128, dtype=np.float32).astype(BF)



def emit_ln(block, sm, X, OUT, scr, stats, eps_ap, wait=(), OUT2=None,
            inplace_src=False, aff=None):
    nt, cc = X.shape[1], X.shape[2]
    sums = stats[:, 0:nt]
    negmean = stats[:, nt:2 * nt]
    vsum = stats[:, 2 * nt:3 * nt]
    rstd = stats[:, 3 * nt:4 * nt]
    nmr = stats[:, 4 * nt:5 * nt]

    @block.vector
    def _(v):
        for s_, val in wait:
            v.wait_ge(s_.h, val)
        for ti in range(nt):
            v.reduce_sum(sums[:, ti:ti + 1], X[:, ti, :], axis=AX.X)
        v.tensor_copy(scr[:, 0:cc], X[:, 0, :])
        sm["v"].inc(v.tensor_scalar(negmean, sums, -1.0 / cc, None, ALU.mult))
    v0 = sm["v"].val

    @block.scalar
    def _(s):
        s.wait_ge(sm["v"].h, v0)
        for ti in range(nt):
            s.activation(scr[:, 0:cc], X[:, ti, :], AF.Square,
                         bias=negmean[:, ti:ti + 1], accum_out=vsum[:, ti:ti + 1])
        s.activation(scr[:, 0:cc], X[:, 0, :], AF.Square, bias=negmean[:, 0:1])
        s.activation(rstd, vsum, AF.Ln, scale=1.0 / cc, bias=eps_ap)
        s.activation(scr[:, 0:cc], X[:, 0, :], AF.Square, bias=negmean[:, 0:1])
        sm["s"].inc(s.activation(rstd, rstd, AF.Exp, scale=-0.5))
    s0 = sm["s"].val

    @block.vector
    def _(v):
        v.wait_ge(sm["s"].h, s0)
        v.tensor_mul(nmr, negmean, rstd)
        v.tensor_copy(scr[:, 0:cc], X[:, 0, :])
        ins = None
        for ti in range(nt):
            if aff is None:
                ins = v.scalar_tensor_tensor(
                    OUT[:, ti, :], X[:, ti, :], rstd[:, ti:ti + 1],
                    nmr[:, ti:ti + 1].broadcast_to([128, cc]), ALU.mult, ALU.add)
            else:
                base_dst = X[:, ti, :] if inplace_src else scr[:, 0:cc]
                v.scalar_tensor_tensor(
                    base_dst, X[:, ti, :], rstd[:, ti:ti + 1],
                    nmr[:, ti:ti + 1].broadcast_to([128, cc]), ALU.mult, ALU.add)
                v.tensor_mul(scr[:, 0:cc] if inplace_src else scr[:, 0:cc],
                             base_dst, aff[0][:, 0:cc])
                ins = v.tensor_add(OUT[:, ti, :], scr[:, 0:cc], aff[1][:, 0:cc])
            if OUT2 is not None:
                ins = v.scalar_tensor_tensor(
                    OUT2[:, ti, :], X[:, ti, :], rstd[:, ti:ti + 1],
                    nmr[:, ti:ti + 1].broadcast_to([128, cc]), ALU.mult, ALU.add)
        sm["v"].inc(ins)
    return sm["v"], sm["v"].val


def emit_tok2chan(block, sm, SRC, DST, IDENT, pst, wait=()):
    nt, ncb = SRC.shape[1], SRC.shape[2] // 128
    pairs = [(cb, ti) for cb in range(ncb) for ti in range(nt)]
    t0 = sm["t"].val
    v0 = sm["v"].val

    @block.tensor
    def _(t):
        for s_, val in wait:
            t.wait_ge(s_.h, val)
        for n, (cb, ti) in enumerate(pairs):
            if n >= 2:
                t.wait_ge(sm["v"].h, v0 + n - 1)
            sm["t"].inc(t.transpose(pst[n % 2][:, :],
                                    SRC[:, ti, cb * 128:(cb + 1) * 128], IDENT[:, :]))

    @block.vector
    def _(v):
        for n, (cb, ti) in enumerate(pairs):
            v.wait_ge(sm["t"].h, t0 + n + 1)
            sm["v"].inc(v.tensor_copy(DST[:, cb, ti * 128:(ti + 1) * 128], pst[n % 2][:, :]))
    return sm["v"], sm["v"].val


def emit_load_x(block, sm, x_dram, X):
    @block.sync
    def _(sy):
        sm["d"].inc(sy.dma_start(out=X[:, :, :],
                                 in_=x_dram.rearrange("(t p) c -> p t c", p=128)), 16)
    return sm["d"], sm["d"].val


def emit_store_y(block, sm, Y, y_dram, wait):
    @block.sync
    def _(sy):
        for s_, val in wait:
            sy.wait_ge(s_.h, val)
        sm["d"].inc(sy.dma_start(out=y_dram.rearrange("(t p) c -> p t c", p=128),
                                 in_=Y[:, :, :]), 16)
    dv = sm["d"].val

    @block.sync
    def _(sy):
        sy.wait_ge(sm["d"].h, dv)



def build_L3f():
    st = Stage("L3f")
    x = st.dram_in("x", [T, C])
    wT = st.dram_in("wT", [C, C])
    sc_r = st.dram_in("sc", [128, C])
    bi_r = st.dram_in("bi", [128, C])
    idn = st.dram_in("idn", [128, 128], BF16)
    y = st.dram_out("y", [T, C])

    with st.es:
        st.alloc_consts()
        X = st.sbuf("X", [128, NT, C])
        LNBF = st.sbuf("LNBF", [128, NT, C], BF16)
        LNT = st.sbuf("LNT", [128, NCB, T], BF16)
        WSB = st.sbuf("WSB", [128, NCB, C], BF16)
        WSG = st.sbuf("WSG", [128, NCB, C], BF16)
        SCB = st.sbuf("SCB", [128, C])
        BIB = st.sbuf("BIB", [128, C])
        IDENT = st.sbuf("IDENT", [128, 128], BF16)
        scr = st.sbuf("scr", [128, C])
        scr2 = st.sbuf("scr2", [128, C])
        stats = st.sbuf("stats", [128, 5 * NT])
        pst = [st.psum(f"pst{i}", [128, 128], BF16) for i in range(2)]
        pmm = [st.psum(f"pmm{i}", [128, 1024]) for i in range(2)]
        sm = st.sems()
        block = st.es.enter_context(st.nc.Block())
        st.load_consts(block, sm)

        @block.sync
        def _(sy):
            sm["d"].inc(sy.dma_start(out=X[:, :, :], in_=x.rearrange("(t p) c -> p t c", p=128)), 16)
            sm["d"].inc(sy.dma_start(out=SCB[:, :], in_=sc_r[:, :]), 16)
            sm["d"].inc(sy.dma_start(out=BIB[:, :], in_=bi_r[:, :]), 16)
            sm["d"].inc(sy.dma_start(out=IDENT[:, :], in_=idn[:, :]), 16)

        @block.gpsimd
        def _(g):
            sm["g"].inc(g.dma_start(out=WSB[:, :, :], in_=wT.rearrange("(cb p) o -> p cb o", p=128)), 16)
        g_w = sm["g"].val
        d_all = sm["d"].val

        @block.vector
        def _(v):
            v.wait_ge(sm["g"].h, g_w)
            ins = None
            for cb in range(NCB):
                v.tensor_scalar(scr2[:, :], WSB[:, cb, :], 0.0, None, ALU.is_ge)
                ins = v.tensor_scalar(WSG[:, cb, :], scr2[:, :], 2.0, -1.0, ALU.mult, ALU.add)
            sm["v"].inc(ins)
        v_sign = sm["v"].val

        _, ln_done = emit_ln(block, sm, X, LNBF, scr, stats, st.cvec(LN_EPS),
                             wait=[(sm["d"], d_all)])
        _, tr_done = emit_tok2chan(block, sm, LNBF, LNT, IDENT, pst,
                                   wait=[(sm["v"], ln_done), (sm["d"], d_all)])

        evac_done = {}
        for ti in range(NT):
            pm = pmm[ti % 2]
            @block.tensor
            def _(t, ti=ti, pm=pm):
                if ti == 0:
                    t.wait_ge(sm["v"].h, tr_done)
                    t.wait_ge(sm["v"].h, v_sign)
                if ti >= 2:
                    t.wait_ge(sm["v"].h, evac_done[ti - 2])
                ins = None
                for cb in range(NCB):
                    for ob in range(2):
                        ins = t.matmul(pm[:, ob * 512:(ob + 1) * 512],
                                       LNT[:, cb, ti * 128:(ti + 1) * 128],
                                       WSG[:, cb, ob * 512:(ob + 1) * 512],
                                       start=(cb == 0), stop=(cb == NCB - 1))
                sm["t"].inc(ins)
            t_mm = sm["t"].val

            @block.vector
            def _(v, ti=ti, pm=pm, t_mm=t_mm):
                v.wait_ge(sm["t"].h, t_mm)
                v.tensor_mul(scr[:, :], pm[:, :], SCB[:, :])
                v.tensor_add(scr[:, :], scr[:, :], BIB[:, :])
                v.tensor_scalar(scr[:, :], scr[:, :], 100.0, -100.0, ALU.min, ALU.max)
                sm["v"].inc(v.tensor_add(X[:, ti, :], scr[:, :], X[:, ti, :]))
            evac_done[ti] = sm["v"].val

        emit_store_y(block, sm, X, y, [(sm["v"], evac_done[NT - 1])])

    def host_prep(b, x_shard, inputs):
        return {
            "x": x_shard,
            "wT": np.ascontiguousarray(inputs["L3_ffn_weights"].T),
            "sc": rep128(inputs["L3_ffn_scale"]),
            "bi": rep128(inputs["L3_ffn_bias"]),
            "idn": IDENT128,
            "cvals": st.const_array(),
        }
    st.host_prep = host_prep
    return st



_BUILDERS = {"L3f": build_L3f}
_CACHE = {}


def get_stage(name):
    if name not in _CACHE:
        _CACHE[name] = _BUILDERS[name]()
    return _CACHE[name]


STAGE_ORDER = ["L0a", "L0f", "L1a", "L1f", "L2a", "L2f", "L3a", "L3f"]


def run_stage(name, x_full, inputs):
    if name == "L0a":
        return run_L0a(x_full, inputs)
    stg = get_stage(name)
    shared = stg.host_prep(0, np.ascontiguousarray(x_full[0]), inputs)
    in_maps = [dict(shared, x=np.ascontiguousarray(x_full[b])) for b in range(8)]
    res = run_bass_kernel_spmd(stg.nc, in_maps, core_ids=list(range(8)))
    return np.stack([np.asarray(r["y"], np.float32) for r in res.results])


def kernel(**inputs):
    x = np.asarray(inputs["L0_attn_x"], np.float32)
    for name in STAGE_ORDER:
        x = run_stage(name, x, inputs)
    return x



def build_L0f():
    st = Stage("L0f")
    x = st.dram_in("x", [T, C])
    w1T = st.dram_in("w1T", [C, 4096])
    w2T = st.dram_in("w2T", [C, 4096])
    h1T = st.dram_in("h1T", [4096, 4096])
    h2T = st.dram_in("h2T", [4096, 4096])
    w3T = st.dram_in("w3T", [4096, C])
    s1c = st.dram_in("s1c", [128, 32])
    s2c = st.dram_in("s2c", [128, 32])
    sh1c = st.dram_in("sh1c", [128, 32])
    sh2c = st.dram_in("sh2c", [128, 32])
    s3r = st.dram_in("s3r", [128, C])
    idn = st.dram_in("idn", [128, 128], BF16)
    y = st.dram_out("y", [T, C])

    TH = T // 2
    NTH = TH // 128

    with st.es:
        st.alloc_consts()
        AR1 = st.sbuf("AR1", [128, 16384])
        AR2 = st.sbuf("AR2", [128, 16384])
        LNBF = st.sbuf("LNBF", [128, NT, C], BF16)
        WBUF = st.sbuf("WBUF", [128, 2, 8192], BF16)
        WB3 = st.sbuf("WB3", [128, 2, 1024], BF16)
        SCL = st.sbuf("SCL", [128, 4, 32])
        S3R = st.sbuf("S3R", [128, C])
        IDENT = st.sbuf("IDENT", [128, 128], BF16)
        scr = st.sbuf("scr", [128, C])
        stats = st.sbuf("stats", [128, 5 * NT])
        PST = [st.psum(f"pstA{i}", [128, 1024], BF16) for i in range(2)]
        PP = [st.psum(f"ppA{i}", [128, 1024]) for i in range(3)]
        sm = st.sems()
        block = st.es.enter_context(st.nc.Block())
        st.load_consts(block, sm)

        X = AR1.rearrange("p (t c) -> p t c", c=C)
        scr2 = WB3.rearrange("p a b -> p (a b)").bitcast(F32)
        LNT = AR2.bitcast(BF16).rearrange("p (cb t) -> p cb t", t=TH)
        Hh = AR1.bitcast(BF16).rearrange("p (ob t) -> p ob t", t=TH)
        G1 = AR2.bitcast(BF16).rearrange("p (ob t) -> p ob t", t=TH)
        G2 = Hh
        AR2f = AR2.rearrange("p (t c) -> p t c", c=C)
        YACC = AR2f
        XH = AR2f

        @block.sync
        def _(sy):
            sm["d"].inc(sy.dma_start(out=X[:, :, :], in_=x.rearrange("(t p) c -> p t c", p=128)), 16)
            sm["d"].inc(sy.dma_start(out=IDENT[:, :], in_=idn[:, :]), 16)
            sm["d"].inc(sy.dma_start(out=S3R[:, :], in_=s3r[:, :]), 16)
            for i, t_ in enumerate([s1c, s2c, sh1c, sh2c]):
                sm["d"].inc(sy.dma_start(out=SCL[:, i, :], in_=t_[:, :]), 16)
        d_all = sm["d"].val

        _, ln_done = emit_ln(block, sm, X, LNBF, scr, stats, st.cvec(LN_EPS),
                             wait=[(sm["d"], d_all)])

        w1v = w1T.rearrange("(cb p) o -> p cb o", p=128)
        w2v = w2T.rearrange("(cb p) o -> p cb o", p=128)
        h1v = h1T.rearrange("(cb p) o -> p cb o", p=128)
        h2v = h2T.rearrange("(cb p) o -> p cb o", p=128)
        w3v = w3T.rearrange("(hb p) d -> p hb d", p=128)

        prev_done = (sm["v"], ln_done)
        for th in range(2):
            lnb_half = LNBF[:, th * NTH:(th + 1) * NTH, :]
            _, tr_done = emit_tok2chan(block, sm, lnb_half,
                                       LNT[:, 0:NCB, :], IDENT,
                                       [PST[0][:, 0:128], PST[1][:, 0:128]],
                                       wait=[prev_done])

            wb_up = WBUF.rearrange("p par (m cb o) -> p par m cb o", m=2, o=512)
            g_slab = {}
            t_slab = {}
            v_read = {}
            s_act = {}
            h_done = {}
            for sl in range(8):
                par = sl % 2

                @block.gpsimd
                def _(g, sl=sl, par=par):
                    if sl == 0:
                        g.wait_ge(sm["v"].h, tr_done)
                    if sl >= 2:
                        g.wait_ge(sm["t"].h, t_slab[sl - 2])
                    sm["g"].inc(g.dma_start(out=wb_up[:, par, 0, :, :],
                                            in_=w1v[:, :, sl * 512:(sl + 1) * 512]), 16)
                    sm["g"].inc(g.dma_start(out=wb_up[:, par, 1, :, :],
                                            in_=w2v[:, :, sl * 512:(sl + 1) * 512]), 16)
                g_slab[sl] = sm["g"].val

                for osub in range(4):
                    ob = sl * 4 + osub

                    @block.tensor
                    def _(t, sl=sl, par=par, osub=osub, ob=ob):
                        if osub == 0:
                            t.wait_ge(sm["g"].h, g_slab[sl])
                            if sl == 0:
                                t.wait_ge(sm["v"].h, tr_done)
                        if ob >= 1:
                            t.wait_ge(sm["v"].h, h_done[ob - 1])
                        ins = None
                        for m, pp in ((0, PP[0]), (1, PP[1])):
                            for cb in range(NCB):
                                for f in range(2):
                                    ins = t.matmul(pp[:, f * 512:(f + 1) * 512],
                                                   wb_up[:, par, m, cb, osub * 128:(osub + 1) * 128],
                                                   LNT[:, cb, f * 512:(f + 1) * 512],
                                                   start=(cb == 0), stop=(cb == NCB - 1))
                        sm["t"].inc(ins)
                    t_mm = sm["t"].val
                    if osub == 3:
                        t_slab[sl] = t_mm

                    @block.vector
                    def _(v, ob=ob, t_mm=t_mm):
                        v.wait_ge(sm["t"].h, t_mm)
                        if ob >= 1:
                            v.wait_ge(sm["s"].h, s_act[ob - 1])
                        sm["v"].inc(v.tensor_scalar(scr[:, :], PP[0][:, :],
                                                    SCL[:, 0, ob:ob + 1], None, ALU.mult))
                    v_a = sm["v"].val

                    @block.scalar
                    def _(s, v_a=v_a):
                        s.wait_ge(sm["v"].h, v_a)
                        sm["s"].inc(s.activation(scr2[:, :], scr[:, :], AF.Silu))
                    s_act[ob] = sm["s"].val

                    @block.vector
                    def _(v, ob=ob):
                        v.wait_ge(sm["s"].h, s_act[ob])
                        sm["v"].inc(v.scalar_tensor_tensor(
                            Hh[:, ob, :], PP[1][:, :], SCL[:, 1, ob:ob + 1],
                            scr2[:, :], ALU.mult, ALU.mult))
                    h_done[ob] = sm["v"].val

            wb = WBUF.rearrange("p par (cb o) -> p par cb o", o=256)
            src_ready = (sm["v"], h_done[31])
            wbuf_free = (sm["v"], h_done[31])
            for phase, SRC, DST, wsrc, sidx in ((0, Hh, G1, h1v, 2), (1, G1, G2, h2v, 3)):
                g_slab2 = {}
                t_slab2 = {}
                v_read2 = {}
                s_done2 = {}
                for sl in range(16):
                    par = sl % 2

                    @block.gpsimd
                    def _(g, sl=sl, par=par, wbuf_free=wbuf_free):
                        if sl == 0:
                            g.wait_ge(wbuf_free[0].h, wbuf_free[1])
                        if sl >= 2:
                            g.wait_ge(sm["t"].h, t_slab2[sl - 2])
                        sm["g"].inc(g.dma_start(out=wb[:, par, :, :],
                                                in_=wsrc[:, :, sl * 256:(sl + 1) * 256]), 16)
                    g_slab2[sl] = sm["g"].val

                    for osub in range(2):
                        ob = sl * 2 + osub

                        @block.tensor
                        def _(t, sl=sl, par=par, osub=osub, ob=ob, src_ready=src_ready):
                            if osub == 0:
                                t.wait_ge(sm["g"].h, g_slab2[sl])
                            if sl == 0 and osub == 0:
                                t.wait_ge(src_ready[0].h, src_ready[1])
                            if ob >= 2:
                                t.wait_ge(sm["v"].h, v_read2[ob - 2])
                            pp = PP[ob % 2]
                            ins = None
                            for cb in range(32):
                                for f in range(2):
                                    ins = t.matmul(pp[:, f * 512:(f + 1) * 512],
                                                   wb[:, par, cb, osub * 128:(osub + 1) * 128],
                                                   SRC[:, cb, f * 512:(f + 1) * 512],
                                                   start=(cb == 0), stop=(cb == 31))
                            sm["t"].inc(ins)
                        t_mm = sm["t"].val
                        if osub == 1:
                            t_slab2[sl] = t_mm

                        @block.vector
                        def _(v, ob=ob, t_mm=t_mm, sidx=sidx):
                            v.wait_ge(sm["t"].h, t_mm)
                            if ob >= 1:
                                v.wait_ge(sm["s"].h, s_done2[ob - 1])
                            sm["v"].inc(v.tensor_scalar(scr[:, :], PP[ob % 2][:, :],
                                                        SCL[:, sidx, ob:ob + 1], None, ALU.mult))
                        v_read2[ob] = sm["v"].val

                        @block.scalar
                        def _(s, ob=ob, DST=DST):
                            s.wait_ge(sm["v"].h, v_read2[ob])
                            sm["s"].inc(s.activation(DST[:, ob, :], scr[:, :], AF.Gelu))
                        s_done2[ob] = sm["s"].val
                src_ready = (sm["s"], s_done2[31])
                wbuf_free = (sm["t"], t_slab2[15])
            g2_done = src_ready

            @block.vector
            def _(v, g2_done=g2_done):
                v.wait_ge(g2_done[0].h, g2_done[1])
                sm["v"].inc(v.memset(AR2f[:, 0:NTH, :], 0.0))
            v_zero = sm["v"].val

            @block.sync
            def _(sy, th=th):
                sy.wait_ge(sm["v"].h, v_zero)
                sm["d"].inc(sy.dma_start(
                    out=AR2f[:, NTH:2 * NTH, :],
                    in_=x.rearrange("(t p) c -> p t c", p=128)[:, th * NTH:(th + 1) * NTH, :]), 16)
            d_xh = sm["d"].val

            t_hb = {}
            v_add = {}
            nlin = 0
            for hb in range(32):
                par = hb % 2

                @block.gpsimd
                def _(g, hb=hb, par=par):
                    if hb == 0:
                        g.wait_ge(sm["v"].h, v_zero)
                    if hb >= 2:
                        g.wait_ge(sm["t"].h, t_hb[hb - 2])
                    sm["g"].inc(g.dma_start(out=WB3[:, par, :], in_=w3v[:, hb, :]), 16)
                g_now = sm["g"].val

                for ti in range(NTH):
                    @block.tensor
                    def _(t, hb=hb, par=par, ti=ti, g_now=g_now, nlin=nlin):
                        if ti == 0:
                            t.wait_ge(sm["g"].h, g_now)
                        if hb == 0 and ti == 0:
                            t.wait_ge(sm["v"].h, v_zero)
                        if nlin >= 2:
                            t.wait_ge(sm["v"].h, v_add[nlin - 2])
                        pp = PP[nlin % 2]
                        ins = None
                        for f in range(2):
                            ins = t.matmul(pp[:, f * 512:(f + 1) * 512],
                                           G2[:, hb, ti * 128:(ti + 1) * 128],
                                           WB3[:, par, f * 512:(f + 1) * 512],
                                           start=True, stop=True)
                        sm["t"].inc(ins)
                    t_mm = sm["t"].val
                    if ti == NTH - 1:
                        t_hb[hb] = t_mm

                    @block.vector
                    def _(v, ti=ti, t_mm=t_mm, nlin=nlin):
                        v.wait_ge(sm["t"].h, t_mm)
                        sm["v"].inc(v.tensor_add(AR2f[:, ti, :], AR2f[:, ti, :],
                                                 PP[nlin % 2][:, :]))
                    v_add[nlin] = sm["v"].val
                    nlin += 1

            @block.vector
            def _(v):
                v.wait_ge(sm["d"].h, d_xh)
                ins = None
                for ti in range(NTH):
                    v.tensor_mul(scr[:, :], AR2f[:, ti, :], S3R[:, :])
                    ins = v.tensor_add(AR2f[:, NTH + ti, :], scr[:, :], AR2f[:, NTH + ti, :])
                sm["v"].inc(ins)
            v_fin = sm["v"].val

            @block.sync
            def _(sy, th=th, v_fin=v_fin):
                sy.wait_ge(sm["v"].h, v_fin)
                sm["d"].inc(sy.dma_start(
                    out=y.rearrange("(t p) c -> p t c", p=128)[:, th * NTH:(th + 1) * NTH, :],
                    in_=AR2f[:, NTH:2 * NTH, :]), 16)
            prev_done = (sm["d"], sm["d"].val)

        @block.sync
        def _(sy):
            sy.wait_ge(sm["d"].h, sm["d"].val)

    def host_prep(b, x_shard, inputs):
        return {
            "x": x_shard,
            "w1T": np.ascontiguousarray(inputs["L0_ffn_w1_w"].T),
            "w2T": np.ascontiguousarray(inputs["L0_ffn_w2_w"].T),
            "h1T": np.ascontiguousarray(inputs["L0_ffn_h1_w"].T),
            "h2T": np.ascontiguousarray(inputs["L0_ffn_h2_w"].T),
            "w3T": np.ascontiguousarray(inputs["L0_ffn_w3_w"].T),
            "s1c": np.ascontiguousarray(inputs["L0_ffn_w1_s"].reshape(32, 128).T),
            "s2c": np.ascontiguousarray(inputs["L0_ffn_w2_s"].reshape(32, 128).T),
            "sh1c": np.ascontiguousarray(inputs["L0_ffn_h1_s"].reshape(32, 128).T),
            "sh2c": np.ascontiguousarray(inputs["L0_ffn_h2_s"].reshape(32, 128).T),
            "s3r": rep128(inputs["L0_ffn_w3_s"]),
            "idn": IDENT128,
            "cvals": st.const_array(),
        }
    st.host_prep = host_prep
    return st


_BUILDERS["L0f"] = build_L0f



def build_L2f(ne=8, dbg=False):
    st = Stage("L2f")
    x = st.dram_in("x", [T, C])
    gwT = st.dram_in("gwT", [C, 12])
    ebr = st.dram_in("ebr", [128, 12])
    w1T = st.dram_in("w1T", [8, C, 2048])
    w2T = st.dram_in("w2T", [8, 2048, C])
    idn = st.dram_in("idn", [128, 128], BF16)
    y = st.dram_out("y", [T, C])
    if dbg:
        dbg_we = st.dram_out("dbg_we", [T, 12])
        dbg_wz = st.dram_out("dbg_wz", [T, 1])
        dbg_ln = st.dram_out("dbg_ln", [T, C], BF16)
        dbg_lnt = st.dram_out("dbg_lnt", [C, T], BF16)
        dbg_sc = st.dram_out("dbg_sc", [T, 48])
        dbg_gw = st.dram_out("dbg_gw", [C, 12], BF16)

    TH = T // 2
    NTH = TH // 128
    NFB = 16

    with st.es:
        st.alloc_consts()
        LNBF = st.sbuf("LNBF", [128, NT, C], BF16)
        LNT = st.sbuf("LNT", [128, NCB, T], BF16)
        HB = st.sbuf("HB", [128, NFB, TH], BF16)
        XH = st.sbuf("XH", [128, NTH, C])
        YA = st.sbuf("YA", [128, NTH, C])
        WUP = st.sbuf("WUP", [128, 2, 8, 512], BF16)
        GWT = st.sbuf("GWT", [128, NCB, 12], BF16)
        EBR = st.sbuf("EBR", [128, 12])
        WE = st.sbuf("WE", [128, NT, 12])
        WZ1 = st.sbuf("WZ1", [128, NT])
        MSK = st.sbuf("MSK", [128, 12], U8)
        SCD = st.sbuf("SCD", [128, NT, 48])
        IDENT = st.sbuf("IDENT", [128, 128], BF16)
        scr = st.sbuf("scr", [128, C])
        rt = st.sbuf("rt", [128, 64])
        stats = st.sbuf("stats", [128, 5 * NT])
        W2SB = XH.rearrange("p a b -> p (a b)").bitcast(BF16).rearrange(
            "p (fb d) -> p fb d", d=C)
        PST = [st.psum(f"pstB{i}", [128, 128], BF16) for i in range(2)]
        PR = st.psum("prB", [128, 16])
        PACC = [st.psum(f"paB{i}", [128, 1024]) for i in range(2)]
        sm = st.sems()
        block = st.es.enter_context(st.nc.Block())
        st.load_consts(block, sm)

        @block.sync
        def _(sy):
            xv = x.rearrange("(t p) c -> p t c", p=128)
            sm["d"].inc(sy.dma_start(out=XH[:, :, :], in_=xv[:, 0:NTH, :]), 16)
            sm["d"].inc(sy.dma_start(out=YA[:, :, :], in_=xv[:, NTH:NT, :]), 16)
            sm["d"].inc(sy.dma_start(out=IDENT[:, :], in_=idn[:, :]), 16)
            sm["d"].inc(sy.dma_start(out=EBR[:, :], in_=ebr[:, :]), 16)

        @block.gpsimd
        def _(g):
            sm["g"].inc(g.dma_start(out=GWT[:, :, :], in_=gwT.rearrange("(cb p) e -> p cb e", p=128)), 16)
        g_gw = sm["g"].val
        d_all = sm["d"].val

        _, ln1 = emit_ln(block, sm, XH, LNBF[:, 0:NTH, :], scr, stats, st.cvec(LN_EPS),
                         wait=[(sm["d"], d_all)])
        _, ln2 = emit_ln(block, sm, YA, LNBF[:, NTH:NT, :], scr, stats, st.cvec(LN_EPS),
                         wait=[(sm["v"], ln1)])
        _, tr_done = emit_tok2chan(block, sm, LNBF, LNT, IDENT, PST,
                                   wait=[(sm["v"], ln2)])

        negb = st.cvec(NEG_BIG)
        zero = st.cvec(0.0)
        SC = SCD
        ev = {}
        for ti in range(NT):
            @block.tensor
            def _(t, ti=ti):
                if ti == 0:
                    t.wait_ge(sm["v"].h, tr_done)
                    t.wait_ge(sm["g"].h, g_gw)
                if ti >= 1:
                    t.wait_ge(sm["v"].h, ev[ti - 1])
                ins = None
                for cb in range(NCB):
                    ins = t.matmul(PR[:, 0:12], LNT[:, cb, ti * 128:(ti + 1) * 128],
                                   GWT[:, cb, :], start=(cb == 0), stop=(cb == NCB - 1))
                sm["t"].inc(ins)
            t_mm = sm["t"].val

            @block.vector
            def _(v, ti=ti, t_mm=t_mm):
                v.wait_ge(sm["t"].h, t_mm)
                sm["v"].inc(v.tensor_add(SC[:, ti, 0:12], PR[:, 0:12], EBR[:, :]))
            ev[ti] = sm["v"].val

        SCv = SC.rearrange("p t k -> p (t k)")
        s12 = SC[:, :, 0:12]
        m1 = SC[:, :, 12]
        m2 = SC[:, :, 13]
        dd = SC[:, :, 14]
        winv = SC[:, :, 15]
        msk3 = SC[:, :, 16:28]
        expd = SC[:, :, 28:40]
        sneg = SC[:, :, 40:44]
        MSKB = st.sbuf("MSKB", [128, NT, 12], U8)
        SNEG = st.sbuf("SNEG", [128, NT, 12])
        DIFF = st.sbuf("DIFF", [128, NT, 12])

        @block.vector
        def _(v):
            v.wait_ge(sm["v"].h, ev[NT - 1])
            v.tensor_copy(scr[:, 0:512], LNBF[:, 0, 0:512])
            v.reduce_max(m1, s12, axis=AX.X)
            v.tensor_copy(scr[:, 0:512], LNBF[:, 0, 0:512])
            v.tensor_tensor(MSKB[:, :, :], s12,
                            m1.rearrange("p t -> p t 1").broadcast_to([128, NT, 12]) if False else SC[:, :, 12:13].broadcast_to([128, NT, 12]),
                            ALU.is_ge)
            v.tensor_copy(scr[:, 0:512], LNBF[:, 0, 0:512])
            v.select(SNEG[:, :, :], MSKB[:, :, :], negb.unsqueeze(2).broadcast_to([128, NT, 12]), s12)
            v.tensor_copy(scr[:, 0:512], LNBF[:, 0, 0:512])
            v.reduce_max(m2, SNEG[:, :, :], axis=AX.X)
            v.tensor_sub(DIFF[:, :, :], s12, SC[:, :, 12:13].broadcast_to([128, NT, 12]))
            sm["v"].inc(v.tensor_sub(dd, m2, m1))
        v_r1 = sm["v"].val

        @block.scalar
        def _(s):
            s.wait_ge(sm["v"].h, v_r1)
            s.activation(expd.rearrange("p t k -> p (t k)") if False else SC[:, :, 28:40], DIFF[:, :, :], AF.Exp)
            sm["s"].inc(s.activation(dd, dd, AF.Exp))
        s_r = sm["s"].val

        @block.vector
        def _(v):
            v.wait_ge(sm["s"].h, s_r)
            v.tensor_scalar(dd, dd, 1.0, None, ALU.add)
            v.tensor_copy(scr[:, 0:512], LNBF[:, 0, 0:512])
            v.reciprocal(winv, dd)
            v.tensor_tensor(MSKB[:, :, :], s12, SC[:, :, 13:14].broadcast_to([128, NT, 12]), ALU.is_ge)
            v.tensor_copy(scr[:, 0:512], LNBF[:, 0, 0:512])
            v.select(SNEG[:, :, :], MSKB[:, :, :], SC[:, :, 28:40],
                     zero.unsqueeze(2).broadcast_to([128, NT, 12]))
            v.tensor_copy(scr[:, 0:512], LNBF[:, 0, 0:512])
            v.tensor_mul(WE[:, :, :], SNEG[:, :, :], SC[:, :, 15:16].broadcast_to([128, NT, 12]))
            v.tensor_copy(scr[:, 0:512], LNBF[:, 0, 0:512])
            sm["v"].inc(v.reduce_sum(WZ1[:, :].unsqueeze(2), WE[:, :, 8:12], axis=AX.X))
        route_done = sm["v"].val
        w1v = w1T.rearrange("e (cb p) f -> e p cb f", p=128)
        w2v = w2T.rearrange("e (fb p) d -> e p fb d", p=128)
        xv = x.rearrange("(t p) c -> p t c", p=128)
        st_done = None
        kup = 0
        kfb = 0
        g_up = {}
        t_up = {}
        s_h = {}
        v_acc_last = route_done
        for th in range(2):
            if th == 1:
                @block.sync
                def _(sy, st_done=st_done):
                    sy.wait_ge(sm["d"].h, st_done)
                    sm["d"].inc(sy.dma_start(out=XH[:, :, :], in_=xv[:, NTH:NT, :]), 16)
            d_x = sm["d"].val

            @block.vector
            def _(v, th=th, d_x=d_x):
                v.wait_ge(sm["d"].h, d_x)
                if th == 0:
                    v.wait_ge(sm["v"].h, route_done)
                ins = None
                for ti in range(NTH):
                    v.tensor_scalar(scr[:, :], LNBF[:, th * NTH + ti, :],
                                    WZ1[:, th * NTH + ti:th * NTH + ti + 1], None, ALU.mult)
                    ins = v.tensor_add(YA[:, ti, :], scr[:, :], XH[:, ti, :])
                sm["v"].inc(ins)
            v_init = sm["v"].val
            prev_dn = v_init

            for e in range(ne):
                @block.gpsimd
                def _(g, e=e, th=th, v_init=v_init):
                    g.wait_ge(sm["v"].h, v_init)
                    if e > 0 or th > 0:
                        g.wait_ge(sm["t"].h, t_dn_last[0])
                    sm["g"].inc(g.dma_start(out=W2SB[:, :, :], in_=w2v[e, :, :, :]), 16)
                g_w2 = sm["g"].val

                for sl in range(4):
                    par = kup % 2
                    @block.gpsimd
                    def _(g, e=e, sl=sl, par=par, kup=kup, v_init=v_init):
                        if kup == 0:
                            g.wait_ge(sm["v"].h, v_init)
                        if kup >= 2:
                            g.wait_ge(sm["t"].h, t_up[kup - 2])
                        sm["g"].inc(g.dma_start(out=WUP[:, par, :, :],
                                                in_=w1v[e, :, :, sl * 512:(sl + 1) * 512]), 16)
                    g_up[kup] = sm["g"].val

                    for osub in range(4):
                        fb = sl * 4 + osub
                        @block.tensor
                        def _(t, e=e, par=par, osub=osub, fb=fb, th=th,
                              kup=kup, kfb=kfb, v_init=v_init, prev_dn=prev_dn):
                            if osub == 0:
                                t.wait_ge(sm["g"].h, g_up[kup])
                                if kup == 0:
                                    t.wait_ge(sm["v"].h, v_init)
                            if fb == 0:
                                t.wait_ge(sm["v"].h, prev_dn)
                            if kfb >= 2:
                                t.wait_ge(sm["s"].h, s_h[kfb - 2])
                            pp = PACC[fb % 2]
                            ins = None
                            for cb in range(NCB):
                                for f in range(2):
                                    ins = t.matmul(pp[:, f * 512:(f + 1) * 512],
                                                   WUP[:, par, cb, osub * 128:(osub + 1) * 128],
                                                   LNT[:, cb, th * TH + f * 512:th * TH + (f + 1) * 512],
                                                   start=(cb == 0), stop=(cb == NCB - 1))
                            sm["t"].inc(ins)
                        t_mm = sm["t"].val
                        if osub == 3:
                            t_up[kup] = t_mm

                        @block.scalar
                        def _(s, fb=fb, t_mm=t_mm):
                            s.wait_ge(sm["t"].h, t_mm)
                            sm["s"].inc(s.activation(HB[:, fb, :], PACC[fb % 2][:, :], AF.Silu))
                        s_h[kfb] = sm["s"].val
                        kfb += 1
                    kup += 1

                h_ready = s_h[kfb - 1]
                v_dn = {}
                t_dn_last = [None]
                for ti in range(NTH):
                    @block.tensor
                    def _(t, e=e, ti=ti, th=th, g_w2=g_w2, h_ready=h_ready):
                        if ti == 0:
                            t.wait_ge(sm["g"].h, g_w2)
                            t.wait_ge(sm["s"].h, h_ready)
                        if ti >= 2:
                            t.wait_ge(sm["v"].h, v_dn[ti - 2])
                        pp = PACC[ti % 2]
                        ins = None
                        for fb in range(NFB):
                            for f in range(2):
                                ins = t.matmul(pp[:, f * 512:(f + 1) * 512],
                                               HB[:, fb, ti * 128:(ti + 1) * 128],
                                               W2SB[:, fb, f * 512:(f + 1) * 512],
                                               start=(fb == 0), stop=(fb == NFB - 1))
                        sm["t"].inc(ins)
                    t_mm = sm["t"].val
                    if ti == NTH - 1:
                        t_dn_last[0] = t_mm

                    @block.vector
                    def _(v, e=e, ti=ti, th=th, t_mm=t_mm):
                        v.wait_ge(sm["t"].h, t_mm)
                        sm["v"].inc(v.scalar_tensor_tensor(
                            YA[:, ti, :], PACC[ti % 2][:, :],
                            WE[:, th * NTH + ti, e:e + 1], YA[:, ti, :],
                            ALU.mult, ALU.add))
                    v_dn[ti] = sm["v"].val
                prev_dn = v_dn[NTH - 1]

            @block.sync
            def _(sy, th=th, prev_dn=prev_dn):
                sy.wait_ge(sm["v"].h, prev_dn)
                sm["d"].inc(sy.dma_start(out=y.rearrange("(t p) c -> p t c", p=128)[:, th * NTH:(th + 1) * NTH, :],
                                         in_=YA[:, :, :]), 16)
            st_done = sm["d"].val

        @block.sync
        def _(sy):
            sy.wait_ge(sm["d"].h, sm["d"].val)

    def host_prep(b, x_shard, inputs):
        return {
            "x": x_shard,
            "gwT": np.ascontiguousarray(inputs["L2_ffn_gate_w"].T),
            "ebr": rep128(inputs["L2_ffn_expert_bias"]),
            "w1T": np.ascontiguousarray(inputs["L2_ffn_w1"].transpose(0, 2, 1)),
            "w2T": np.ascontiguousarray(inputs["L2_ffn_w2"].transpose(0, 2, 1)),
            "idn": IDENT128,
            "cvals": st.const_array(),
        }
    st.host_prep = host_prep
    return st


_BUILDERS["L2f"] = build_L2f



def build_L1f():
    st = Stage("L1f")
    x = st.dram_in("x", [T, C])
    wq = st.dram_in("wq", [C, C])
    wk = st.dram_in("wk", [C, C])
    wv = st.dram_in("wv", [C, C])
    wo = st.dram_in("wo", [C, C])
    bo_r = st.dram_in("bo_r", [128, C])
    g1_r = st.dram_in("g1_r", [128, C])
    b1_r = st.dram_in("b1_r", [128, C])
    g2_r = st.dram_in("g2_r", [128, C])
    b2_r = st.dram_in("b2_r", [128, C])
    gw = st.dram_in("gw", [C, 8])
    w1 = st.dram_in("w1", [8, C, 2048])
    w2 = st.dram_in("w2", [8, C, 2048])
    w3 = st.dram_in("w3", [8, 2048, C])
    idn = st.dram_in("idn", [128, 128], BF16)
    trilm = st.dram_in("trilm", [128, 128])
    y = st.dram_out("y", [T, C])

    TH = T // 2
    NTH = TH // 128
    NFB = 16

    with st.es:
        st.alloc_consts()
        Z = st.sbuf("Z", [128, NT, C])
        BUF_B = st.sbuf("BUF_B", [128, NT, C], BF16)
        BUF_C = st.sbuf("BUF_C", [128, NCB, T], BF16)
        BUF_F = st.sbuf("BUF_F", [128, NT, C], BF16)
        QH = st.sbuf("QH", [128, T], BF16)
        KH = st.sbuf("KH", [128, T], BF16)
        PSB = st.sbuf("PSB", [128, T], BF16)
        PTT = st.sbuf("PTT", [128, T], BF16)
        WBUF = st.sbuf("WBUF", [128, 2, 8, 256], BF16)
        GWS = st.sbuf("GWS", [128, NCB, 8], BF16)
        WE = st.sbuf("WE", [128, NT, 8])
        SCD = st.sbuf("SCD", [128, NT, 24])
        MSKB = st.sbuf("MSKB", [128, NT, 8], U8)
        SNEG = st.sbuf("SNEG", [128, NT, 8])
        DIFF = st.sbuf("DIFF", [128, NT, 8])
        PARR = st.sbuf("PARR", [128, 2, C])
        scr2 = st.sbuf("scr2l1", [128, C])
        IDENT = st.sbuf("IDENT", [128, 128], BF16)
        TRIL = st.sbuf("TRIL", [128, 128])
        scr = st.sbuf("scr", [128, C])
        stats = st.sbuf("stats", [128, 5 * NT])
        rowst = st.sbuf("rowst", [128, 8])
        PS8 = st.psum("ps8", [128, 2048])
        PR = st.psum("prC", [128, 16])
        PTm = [st.psum(f"ptmC{i}", [128, 128], BF16) for i in range(2)]
        PCTX = st.psum("pctxC", [128, 64])
        sm = st.sems()
        block = st.es.enter_context(st.nc.Block())
        st.load_consts(block, sm)

        BO = PARR[:, 0, :]
        G1 = PARR[:, 0, :]
        B1 = PARR[:, 1, :]
        G2 = PARR[:, 0, :]
        B2 = PARR[:, 1, :]
        H1BF = BUF_B[:, 0:NT, :]
        CTX = BUF_B[:, 0:NT, :]
        H2BF = BUF_B[:, 0:NT, :]
        HMOE = BUF_B.rearrange("p a b -> p (a b)").rearrange("p (fb t) -> p fb t", t=TH)
        XHV = BUF_B.rearrange("p a b -> p (a b)").bitcast(F32).rearrange("p (t c) -> p t c", c=C)
        PA = PS8[:, 0:1024]
        PB = PS8[:, 1024:2048]

        @block.sync
        def _(sy):
            sm["d"].inc(sy.dma_start(out=Z[:, :, :], in_=x.rearrange("(t p) c -> p t c", p=128)), 16)
            sm["d"].inc(sy.dma_start(out=IDENT[:, :], in_=idn[:, :]), 16)
            sm["d"].inc(sy.dma_start(out=TRIL[:, :], in_=trilm[:, :]), 16)
            sm["d"].inc(sy.dma_start(out=PARR[:, 0, :], in_=g1_r[:, :]), 16)
            sm["d"].inc(sy.dma_start(out=PARR[:, 1, :], in_=b1_r[:, :]), 16)
        d_all = sm["d"].val

        @block.gpsimd
        def _(g):
            sm["g"].inc(g.dma_start(out=GWS[:, :, :], in_=gw.rearrange("(cb p) e -> p cb e", p=128)), 16)
        g_gw = sm["g"].val

        _, ln_done = emit_ln(block, sm, Z, H1BF, scr, stats, st.cvec(LN_EPS),
                             wait=[(sm["d"], d_all)], inplace_src=True, aff=(G1, B1))

        @block.sync
        def _(sy):
            sy.wait_ge(sm["v"].h, ln_done)
            sm["d"].inc(sy.dma_start(out=PARR[:, 0, :], in_=bo_r[:, :]), 16)
        d_bo = sm["d"].val

        _, tr1 = emit_tok2chan(block, sm, H1BF, BUF_C, IDENT, PTm, wait=[(sm["v"], ln_done)])

        wvv = wv.rearrange("(cb p) o -> p cb o", p=128)
        wov = wo.rearrange("(cb p) o -> p cb o", p=128)
        v_evac = {}
        t_slab = {}
        for sl in range(4):
            par = sl % 2
            @block.gpsimd
            def _(g, sl=sl, par=par):
                if sl >= 2:
                    g.wait_ge(sm["t"].h, t_slab[sl - 2])
                sm["g"].inc(g.dma_start(out=WBUF[:, par, :, :], in_=wvv[:, :, sl * 256:(sl + 1) * 256]), 16)
            g_now = sm["g"].val
            for ti in range(NT):
                n = sl * NT + ti
                @block.tensor
                def _(t, sl=sl, par=par, ti=ti, n=n, g_now=g_now):
                    if ti == 0:
                        t.wait_ge(sm["g"].h, g_now)
                        if sl == 0:
                            t.wait_ge(sm["v"].h, tr1)
                    if n >= 2:
                        t.wait_ge(sm["v"].h, v_evac[n - 2])
                    pp = PS8[:, (n % 2) * 512:(n % 2) * 512 + 256]
                    ins = None
                    for cb in range(NCB):
                        ins = t.matmul(pp, BUF_C[:, cb, ti * 128:(ti + 1) * 128],
                                       WBUF[:, par, cb, :], start=(cb == 0), stop=(cb == NCB - 1))
                    sm["t"].inc(ins)
                t_mm = sm["t"].val
                if ti == NT - 1:
                    t_slab[sl] = t_mm
                @block.vector
                def _(v, sl=sl, ti=ti, n=n, t_mm=t_mm):
                    v.wait_ge(sm["t"].h, t_mm)
                    sm["v"].inc(v.tensor_copy(BUF_F[:, ti, sl * 256:(sl + 1) * 256],
                                              PS8[:, (n % 2) * 512:(n % 2) * 512 + 256]))
                v_evac[n] = sm["v"].val
        v_done = sm["v"].val

        wqv = wq.rearrange("(cb p) o -> p cb o", p=128)
        wkv = wk.rearrange("(cb p) o -> p cb o", p=128)
        ctx_done = {}
        prev_scores_user = v_done
        for hp in range(8):
            par_hp = hp % 2
            @block.gpsimd
            def _(g, hp=hp, par_hp=par_hp):
                if hp == 0:
                    g.wait_ge(sm["v"].h, v_done)
                else:
                    g.wait_ge(sm["t"].h, qk_mm_done[hp - 1])
                sm["g"].inc(g.dma_start(out=WBUF[:, par_hp, :, 0:128], in_=wqv[:, :, hp * 128:(hp + 1) * 128]), 16)
                sm["g"].inc(g.dma_start(out=WBUF[:, par_hp, :, 128:256], in_=wkv[:, :, hp * 128:(hp + 1) * 128]), 16)
            g_qk = sm["g"].val

            qk_ev = {}
            for m, DSTQK in ((0, QH), (1, KH)):
                for tg in range(4):
                    @block.tensor
                    def _(t, hp=hp, m=m, tg=tg, g_qk=g_qk, par_hp=par_hp):
                        if m == 0 and tg == 0:
                            t.wait_ge(sm["g"].h, g_qk)
                            t.wait_ge(sm["v"].h, prev_scores_user)
                        ins = None
                        for cb in range(NCB):
                            ins = t.matmul(PS8[:, tg * 512:(tg + 1) * 512],
                                           WBUF[:, par_hp, cb, m * 128:(m + 1) * 128],
                                           BUF_C[:, cb, tg * 512:(tg + 1) * 512],
                                           start=(cb == 0), stop=(cb == NCB - 1))
                        sm["t"].inc(ins)
                    t_mm = sm["t"].val
                    @block.vector
                    def _(v, m=m, tg=tg, t_mm=t_mm, DSTQK=DSTQK):
                        v.wait_ge(sm["t"].h, t_mm)
                        if m == 0:
                            sm["v"].inc(v.tensor_scalar(DSTQK[:, tg * 512:(tg + 1) * 512],
                                                        PS8[:, tg * 512:(tg + 1) * 512],
                                                        0.125, None, ALU.mult))
                        else:
                            sm["v"].inc(v.tensor_copy(DSTQK[:, tg * 512:(tg + 1) * 512],
                                                      PS8[:, tg * 512:(tg + 1) * 512]))
                    qk_ev[(m, tg)] = sm["v"].val
            qk_mm_done = qk_mm_done if hp else {}
            qk_mm_done[hp] = sm["t"].val
            qk_done = sm["v"].val

            for hh in range(2):
                h = hp * 2 + hh
                hs = hh * 64
                for qt in range(NT):
                    L = (qt + 1) * 128
                    ng = (L + 511) // 512
                    @block.tensor
                    def _(t, hp=hp, hh=hh, qt=qt, hs=hs, L=L, ng=ng, qk_done=qk_done):
                        if qt == 0 and hh == 0:
                            t.wait_ge(sm["v"].h, qk_done)
                        if not (qt == 0 and hh == 0):
                            t.wait_ge(sm["s"].h, exp_done_prev[0])
                        ins = None
                        for g_ in range(ng):
                            f0 = g_ * 512
                            f1 = min(L, f0 + 512)
                            ins = t.matmul(PS8[:, f0:f1],
                                           QH[hs:hs + 64, qt * 128:(qt + 1) * 128],
                                           KH[hs:hs + 64, f0:f1],
                                           start=True, stop=True)
                        sm["t"].inc(ins)
                    t_sc = sm["t"].val

                    @block.vector
                    def _(v, qt=qt, L=L, t_sc=t_sc):
                        v.wait_ge(sm["t"].h, t_sc)
                        v.tensor_add(PS8[:, qt * 128:(qt + 1) * 128],
                                     PS8[:, qt * 128:(qt + 1) * 128], TRIL[:, :])
                        v.tensor_copy(scr[:, 0:512], Z[:, 0, 0:512])
                        v.reduce_max(rowst[:, 0:1], PS8[:, 0:L], axis=AX.X)
                        v.tensor_copy(scr[:, 0:512], Z[:, 0, 0:512])
                        sm["v"].inc(v.tensor_scalar(rowst[:, 1:2], rowst[:, 0:1], -1.0, None, ALU.mult))
                    v_m = sm["v"].val

                    @block.scalar
                    def _(s, L=L, v_m=v_m):
                        s.wait_ge(sm["v"].h, v_m)
                        sm["s"].inc(s.activation(PSB[:, 0:L], PS8[:, 0:L], AF.Exp, bias=rowst[:, 1:2]))
                    exp_done_prev = [sm["s"].val]
                    s_exp = sm["s"].val

                    @block.vector
                    def _(v, L=L, s_exp=s_exp):
                        v.wait_ge(sm["s"].h, s_exp)
                        v.reduce_sum(rowst[:, 2:3], PSB[:, 0:L], axis=AX.X)
                        v.tensor_copy(scr[:, 0:512], Z[:, 0, 0:512])
                        sm["v"].inc(v.reciprocal(rowst[:, 3:4], rowst[:, 2:3]))
                    v_rs = sm["v"].val

                    nst = qt + 1
                    base_vv = sm["v"].val
                    tr_ev = {n: base_vv + n + 1 for n in range(nst)}
                    @block.tensor
                    def _(t, nst=nst, s_exp=s_exp, tr_ev=tr_ev):
                        t.wait_ge(sm["s"].h, s_exp)
                        for n in range(nst):
                            if n >= 2:
                                t.wait_ge(sm["v"].h, tr_ev[n - 2])
                            sm["t"].inc(t.transpose(PTm[n % 2][:, :],
                                                    PSB[:, n * 128:(n + 1) * 128], IDENT[:, :]))
                    t_tr0 = sm["t"].val - nst

                    @block.vector
                    def _(v, nst=nst, t_tr0=t_tr0):
                        for n in range(nst):
                            v.wait_ge(sm["t"].h, t_tr0 + n + 1)
                            sm["v"].inc(v.tensor_copy(PTT[:, n * 128:(n + 1) * 128], PTm[n % 2][:, :]))
                    assert sm["v"].val == base_vv + nst

                    @block.tensor
                    def _(t, nst=nst, h=h, qt=qt, hs=hs):
                        if ctx_done:
                            t.wait_ge(sm["v"].h, max(ctx_done.values()))
                        ins = None
                        for n in range(nst):
                            t.wait_ge(sm["v"].h, tr_ev[n])
                            ins = t.matmul(PCTX[:, :], PTT[:, n * 128:(n + 1) * 128],
                                           BUF_F[:, n, h * 64:(h + 1) * 64],
                                           start=(n == 0), stop=(n == nst - 1))
                        sm["t"].inc(ins)
                    t_ctx = sm["t"].val

                    @block.vector
                    def _(v, h=h, qt=qt, t_ctx=t_ctx, v_rs=v_rs):
                        v.wait_ge(sm["t"].h, t_ctx)
                        v.wait_ge(sm["v"].h, v_rs)
                        sm["v"].inc(v.tensor_scalar(CTX[:, qt, h * 64:(h + 1) * 64],
                                                    PCTX[:, :], rowst[:, 3:4], None, ALU.mult))
                    ctx_done[(h, qt)] = sm["v"].val
            prev_scores_user = sm["v"].val
        attn_done = sm["v"].val

        _, tr2 = emit_tok2chan(block, sm, CTX, BUF_C, IDENT, PTm, wait=[(sm["v"], attn_done)])

        z2_done = {}
        t_wsl = {}
        for sl in range(4):
            par = sl % 2
            @block.gpsimd
            def _(g, sl=sl, par=par):
                if sl == 0:
                    g.wait_ge(sm["v"].h, tr2)
                if sl >= 2:
                    g.wait_ge(sm["t"].h, t_wsl[sl - 2])
                sm["g"].inc(g.dma_start(out=WBUF[:, par, :, :], in_=wov[:, :, sl * 256:(sl + 1) * 256]), 16)
            g_now = sm["g"].val
            for ti in range(NT):
                n = sl * NT + ti
                @block.tensor
                def _(t, sl=sl, par=par, ti=ti, n=n, g_now=g_now):
                    if ti == 0:
                        t.wait_ge(sm["g"].h, g_now)
                        if sl == 0:
                            t.wait_ge(sm["v"].h, tr2)
                    if n >= 2:
                        t.wait_ge(sm["v"].h, z2_done[n - 2])
                    pp = PS8[:, (n % 2) * 512:(n % 2) * 512 + 256]
                    ins = None
                    for cb in range(NCB):
                        ins = t.matmul(pp, BUF_C[:, cb, ti * 128:(ti + 1) * 128],
                                       WBUF[:, par, cb, :], start=(cb == 0), stop=(cb == NCB - 1))
                    sm["t"].inc(ins)
                t_mm = sm["t"].val
                if ti == NT - 1:
                    t_wsl[sl] = t_mm
                @block.vector
                def _(v, sl=sl, ti=ti, n=n, t_mm=t_mm):
                    v.wait_ge(sm["t"].h, t_mm)
                    if n == 0:
                        v.wait_ge(sm["d"].h, d_bo)
                    cs = slice(sl * 256, (sl + 1) * 256)
                    pp = PS8[:, (n % 2) * 512:(n % 2) * 512 + 256]
                    v.tensor_add(scr[:, 0:256], pp, BO[:, cs])
                    sm["v"].inc(v.tensor_add(Z[:, ti, cs], scr[:, 0:256], Z[:, ti, cs]))
                z2_done[n] = sm["v"].val
        z2_all = sm["v"].val

        @block.sync
        def _(sy):
            sy.wait_ge(sm["v"].h, z2_all)
            sm["d"].inc(sy.dma_start(out=PARR[:, 0, :], in_=g2_r[:, :]), 16)
            sm["d"].inc(sy.dma_start(out=PARR[:, 1, :], in_=b2_r[:, :]), 16)
        d_g2 = sm["d"].val
        _, ln2_done = emit_ln(block, sm, Z, H2BF, scr, stats, st.cvec(LN_EPS),
                              wait=[(sm["v"], z2_all), (sm["d"], d_g2)], aff=(G2, B2))
        _, tr3 = emit_tok2chan(block, sm, H2BF, BUF_C, IDENT, PTm, wait=[(sm["v"], ln2_done)])

        zero = st.cvec(0.0)
        negb = st.cvec(NEG_BIG)
        ev = {}
        for ti in range(NT):
            @block.tensor
            def _(t, ti=ti):
                if ti == 0:
                    t.wait_ge(sm["v"].h, tr3)
                    t.wait_ge(sm["g"].h, g_gw)
                if ti >= 1:
                    t.wait_ge(sm["v"].h, ev[ti - 1])
                ins = None
                for cb in range(NCB):
                    ins = t.matmul(PR[:, 0:8], BUF_C[:, cb, ti * 128:(ti + 1) * 128],
                                   GWS[:, cb, :], start=(cb == 0), stop=(cb == NCB - 1))
                sm["t"].inc(ins)
            t_mm = sm["t"].val
            @block.vector
            def _(v, ti=ti, t_mm=t_mm):
                v.wait_ge(sm["t"].h, t_mm)
                sm["v"].inc(v.tensor_copy(SCD[:, ti, 0:8], PR[:, 0:8]))
            ev[ti] = sm["v"].val

        s8 = SCD[:, :, 0:8]
        @block.vector
        def _(v):
            v.wait_ge(sm["v"].h, ev[NT - 1])
            v.tensor_copy(scr[:, 0:512], Z[:, 0, 0:512])
            v.reduce_max(SCD[:, :, 8], s8, axis=AX.X)
            v.tensor_copy(scr[:, 0:512], Z[:, 0, 0:512])
            v.tensor_tensor(MSKB[:, :, :], s8, SCD[:, :, 8:9].broadcast_to([128, NT, 8]), ALU.is_ge)
            v.tensor_copy(scr[:, 0:512], Z[:, 0, 0:512])
            v.select(SNEG[:, :, :], MSKB[:, :, :], negb[:, :].unsqueeze(2).broadcast_to([128, NT, 8]), s8)
            v.tensor_copy(scr[:, 0:512], Z[:, 0, 0:512])
            v.reduce_max(SCD[:, :, 9], SNEG[:, :, :], axis=AX.X)
            v.tensor_sub(DIFF[:, :, :], s8, SCD[:, :, 8:9].broadcast_to([128, NT, 8]))
            sm["v"].inc(v.tensor_sub(SCD[:, :, 10], SCD[:, :, 9], SCD[:, :, 8]))
        v_r1 = sm["v"].val

        @block.scalar
        def _(s):
            s.wait_ge(sm["v"].h, v_r1)
            s.activation(SCD[:, :, 16:24], DIFF[:, :, :], AF.Exp)
            sm["s"].inc(s.activation(SCD[:, :, 10], SCD[:, :, 10], AF.Exp))
        s_r = sm["s"].val

        @block.vector
        def _(v):
            v.wait_ge(sm["s"].h, s_r)
            v.tensor_scalar(SCD[:, :, 10], SCD[:, :, 10], 1.0, None, ALU.add)
            v.tensor_copy(scr[:, 0:512], Z[:, 0, 0:512])
            v.reciprocal(SCD[:, :, 11], SCD[:, :, 10])
            v.tensor_tensor(MSKB[:, :, :], s8, SCD[:, :, 9:10].broadcast_to([128, NT, 8]), ALU.is_ge)
            v.tensor_copy(scr[:, 0:512], Z[:, 0, 0:512])
            v.select(SNEG[:, :, :], MSKB[:, :, :], SCD[:, :, 16:24],
                     zero[:, :].unsqueeze(2).broadcast_to([128, NT, 8]))
            v.tensor_copy(scr[:, 0:512], Z[:, 0, 0:512])
            sm["v"].inc(v.tensor_mul(WE[:, :, :], SNEG[:, :, :],
                                     SCD[:, :, 11:12].broadcast_to([128, NT, 8])))
        route_done = sm["v"].val

        w1v = w1.rearrange("e (cb p) f -> e p cb f", p=128)
        w2v = w2.rearrange("e (cb p) f -> e p cb f", p=128)
        w3v = w3.rearrange("e (fb p) d -> e p fb d", p=128)
        xv = x.rearrange("(t p) c -> p t c", p=128)
        kup = 0
        kfb = 0
        g_up = {}
        t_up = {}
        v_h = {}
        t_w3last = [None]
        st_done = None
        for th in range(2):
            prev_dn = route_done if th == 0 else st_done_v
            for e in range(8):
                @block.gpsimd
                def _(g, e=e, th=th, prev_dn=prev_dn):
                    if e == 0 and th == 0:
                        g.wait_ge(sm["v"].h, route_done)
                    if t_w3last[0] is not None:
                        g.wait_ge(sm["t"].h, t_w3last[0])
                    sm["g"].inc(g.dma_start(out=BUF_F[:, :, :].rearrange("p a b -> p (a b)").rearrange("p (fb d) -> p fb d", d=C),
                                            in_=w3v[e, :, :, :]), 16)
                g_w3 = sm["g"].val

                for fb in range(NFB):
                    par = kup % 2
                    @block.gpsimd
                    def _(g, e=e, fb=fb, par=par, kup=kup, th=th, prev_dn=prev_dn):
                        if kup == 0:
                            g.wait_ge(sm["v"].h, route_done)
                        if kup >= 2:
                            g.wait_ge(sm["t"].h, t_up[kup - 2])
                        sm["g"].inc(g.dma_start(out=WBUF[:, par, :, 0:128],
                                                in_=w1v[e, :, :, fb * 128:(fb + 1) * 128]), 16)
                        sm["g"].inc(g.dma_start(out=WBUF[:, par, :, 128:256],
                                                in_=w2v[e, :, :, fb * 128:(fb + 1) * 128]), 16)
                    g_up[kup] = sm["g"].val

                    @block.tensor
                    def _(t, e=e, par=par, fb=fb, th=th,
                          kup=kup, kfb=kfb, prev_dn=prev_dn):
                        t.wait_ge(sm["g"].h, g_up[kup])
                        if kfb % NFB == 0:
                            t.wait_ge(sm["v"].h, prev_dn)
                        if kfb >= 1:
                            t.wait_ge(sm["v"].h, v_h[kfb - 1])

                        ins = None
                        for m, pp in ((0, PA), (1, PB)):
                            for cb in range(NCB):
                                for f in range(2):
                                    ins = t.matmul(pp[:, f * 512:(f + 1) * 512],
                                                   WBUF[:, par, cb, m * 128:(m + 1) * 128],
                                                   BUF_C[:, cb, th * TH + f * 512:th * TH + (f + 1) * 512],
                                                   start=(cb == 0), stop=(cb == NCB - 1))
                        sm["t"].inc(ins)
                    t_mm = sm["t"].val
                    t_up[kup] = t_mm

                    @block.scalar
                    def _(s, t_mm=t_mm):
                        s.wait_ge(sm["t"].h, t_mm)
                        sm["s"].inc(s.activation(scr2[:, :], PA, AF.Silu))
                    s_a = sm["s"].val

                    @block.vector
                    def _(v, fb=fb, s_a=s_a):
                        v.wait_ge(sm["s"].h, s_a)
                        sm["v"].inc(v.tensor_mul(HMOE[:, fb, :], PB, scr2[:, :]))
                    v_h[kfb] = sm["v"].val
                    kfb += 1
                    kup += 1

                h_ready = v_h[kfb - 1]
                v_dn = {}
                W3S = BUF_F[:, :, :].rearrange("p a b -> p (a b)").rearrange("p (fb d) -> p fb d", d=C)
                for ti in range(NTH):
                    @block.tensor
                    def _(t, e=e, ti=ti, th=th, g_w3=g_w3, h_ready=h_ready):
                        if ti == 0:
                            t.wait_ge(sm["g"].h, g_w3)
                            t.wait_ge(sm["v"].h, h_ready)
                        if ti >= 2:
                            t.wait_ge(sm["v"].h, v_dn[ti - 2])
                        pp = PA if ti % 2 == 0 else PB
                        ins = None
                        for fb in range(NFB):
                            for f in range(2):
                                ins = t.matmul(pp[:, f * 512:(f + 1) * 512],
                                               HMOE[:, fb, ti * 128:(ti + 1) * 128],
                                               W3S[:, fb, f * 512:(f + 1) * 512],
                                               start=(fb == 0), stop=(fb == NFB - 1))
                        sm["t"].inc(ins)
                    t_mm = sm["t"].val
                    if ti == NTH - 1:
                        t_w3last[0] = t_mm

                    @block.vector
                    def _(v, e=e, ti=ti, th=th, t_mm=t_mm):
                        v.wait_ge(sm["t"].h, t_mm)
                        pp = PA if ti % 2 == 0 else PB
                        sm["v"].inc(v.scalar_tensor_tensor(
                            Z[:, th * NTH + ti, :], pp,
                            WE[:, th * NTH + ti, e:e + 1], Z[:, th * NTH + ti, :],
                            ALU.mult, ALU.add))
                    v_dn[ti] = sm["v"].val
                prev_dn = v_dn[NTH - 1]

            @block.sync
            def _(sy, th=th, prev_dn=prev_dn):
                sy.wait_ge(sm["v"].h, prev_dn)
                sm["d"].inc(sy.dma_start(out=XHV[:, 0:NTH, :], in_=xv[:, th * NTH:(th + 1) * NTH, :]), 16)
            d_xh = sm["d"].val

            @block.vector
            def _(v, th=th, d_xh=d_xh):
                v.wait_ge(sm["d"].h, d_xh)
                ins = None
                for ti in range(NTH):
                    ins = v.tensor_add(Z[:, th * NTH + ti, :], Z[:, th * NTH + ti, :],
                                       XHV[:, ti, :])
                sm["v"].inc(ins)
            st_done_v = sm["v"].val

            @block.sync
            def _(sy, th=th, st_done_v=st_done_v):
                sy.wait_ge(sm["v"].h, st_done_v)
                sm["d"].inc(sy.dma_start(
                    out=y.rearrange("(t p) c -> p t c", p=128)[:, th * NTH:(th + 1) * NTH, :],
                    in_=Z[:, th * NTH:(th + 1) * NTH, :]), 16)
            st_done = sm["d"].val

        @block.sync
        def _(sy):
            sy.wait_ge(sm["d"].h, sm["d"].val)

    def host_prep(b, x_shard, inputs):
        tril = np.where(np.arange(128)[None, :] <= np.arange(128)[:, None], 0.0, NEG_BIG).astype(np.float32)
        return {
            "x": x_shard,
            "wq": np.ascontiguousarray(inputs["L1_ffn_wq"]),
            "wk": np.ascontiguousarray(inputs["L1_ffn_wk"]),
            "wv": np.ascontiguousarray(inputs["L1_ffn_wv"]),
            "wo": np.ascontiguousarray(inputs["L1_ffn_wo"]),
            "bo_r": rep128(inputs["L1_ffn_bo"]),
            "g1_r": rep128(inputs["L1_ffn_ln1_scale"]),
            "b1_r": rep128(inputs["L1_ffn_ln1_shift"]),
            "g2_r": rep128(inputs["L1_ffn_ln2_scale"]),
            "b2_r": rep128(inputs["L1_ffn_ln2_shift"]),
            "gw": np.ascontiguousarray(inputs["L1_ffn_gate_w"]),
            "w1": np.ascontiguousarray(inputs["L1_ffn_w1"]),
            "w2": np.ascontiguousarray(inputs["L1_ffn_w2"]),
            "w3": np.ascontiguousarray(inputs["L1_ffn_w3"]),
            "idn": IDENT128,
            "trilm": tril,
            "cvals": st.const_array(),
        }
    st.host_prep = host_prep
    return st


_BUILDERS["L1f"] = build_L1f



def build_L2a():
    st = Stage("L2a")
    x = st.dram_in("x", [T, C])
    wqT = st.dram_in("wqT", [C, C])
    wkT = st.dram_in("wkT", [C, C])
    wvT = st.dram_in("wvT", [C, C])
    woT = st.dram_in("woT", [C, C])
    fq = st.dram_in("fq", [128, 8, 128])
    fk = st.dram_in("fk", [128, 8, 128])
    triu01 = st.dram_in("triu01", [128, 128])
    gnw = st.dram_in("gnw", [1, C])
    gnb = st.dram_in("gnb", [1, C])
    idn = st.dram_in("idn", [128, 128], BF16)
    y = st.dram_out("y", [T, C])

    H = 16
    GAM = [1.0 - 2.0 ** (-5.0 - h * 0.5) for h in range(H)]

    with st.es:
        st.alloc_consts()
        ARX = st.sbuf("ARX", [128, NT * C])
        X = ARX.rearrange("p (t c) -> p t c", c=C)
        XB = ARX.bitcast(BF16)
        VB = XB[:, 0:16384].rearrange("p (t c) -> p t c", c=C)
        QH = XB[:, 16384:18432]
        KH = XB[:, 18432:20480]
        SST = XB[:, 20480:22528]
        FQ = XB[:, 22528:23552].rearrange("p (a b) -> p a b", b=128)
        FK = XB[:, 23552:24576].rearrange("p (a b) -> p a b", b=128)
        LNBF = st.sbuf("LNBF", [128, NT, C], BF16)
        LNT = st.sbuf("LNT", [128, NCB, T], BF16)
        WBUF = st.sbuf("WBUF", [128, 2, 8, 256], BF16)
        TRI = st.sbuf("TRI", [128, 128])
        GNP = st.sbuf("GNP", [128, 2, NT, 16])
        GROW = st.sbuf("GROW", [1, 4, C])
        GST = st.sbuf("GST", [1, 128])
        ONESR = st.sbuf("ONESR", [1, 128])
        SAB = st.sbuf("SAB", [128, 2, C], BF16)
        IDENT = st.sbuf("IDENT", [128, 128], BF16)
        scr = st.sbuf("scr", [128, C])
        stats = st.sbuf("stats", [128, 5 * NT])
        PS8 = st.psum("ps8b", [128, 2048])
        PTm = [st.psum(f"ptmD{i}", [128, 128], BF16) for i in range(2)]
        PCTX = st.psum("pctxD", [128, 64])
        PROW = st.psum("prowD", [128, 512])
        sm = st.sems()
        block = st.es.enter_context(st.nc.Block())
        st.load_consts(block, sm)

        CTX = LNBF
        CTXT = LNT

        @block.vector
        def _(v):
            sm["v"].inc(v.memset(ONESR[:, :], 1.0))
        v_ones = sm["v"].val

        @block.sync
        def _(sy):
            sm["d"].inc(sy.dma_start(out=X[:, :, :], in_=x.rearrange("(t p) c -> p t c", p=128)), 16)
            sm["d"].inc(sy.dma_start(out=IDENT[:, :], in_=idn[:, :]), 16)
            sm["d"].inc(sy.dma_start(out=TRI[:, :], in_=triu01[:, :]), 16)
            sm["d"].inc(sy.dma_start(out=GROW[0:1, 2, :], in_=gnw[:, :]), 16)
            sm["d"].inc(sy.dma_start(out=GROW[0:1, 3, :], in_=gnb[:, :]), 16)
        d_all = sm["d"].val

        _, ln_done = emit_ln(block, sm, X, LNBF, scr, stats, st.cvec(LN_EPS),
                             wait=[(sm["d"], d_all)])
        @block.gpsimd
        def _(g):
            g.wait_ge(sm["v"].h, ln_done)
            sm["g"].inc(g.dma_start(out=FQ[:, :, :], in_=fq[:, :, :]), 16)
            sm["g"].inc(g.dma_start(out=FK[:, :, :], in_=fk[:, :, :]), 16)
        g_fqk = sm["g"].val
        _, tr1 = emit_tok2chan(block, sm, LNBF, LNT, IDENT, PTm, wait=[(sm["v"], ln_done)])

        wvv = wvT.rearrange("(cb p) o -> p cb o", p=128)
        wov = woT.rearrange("(cb p) o -> p cb o", p=128)
        wqv = wqT.rearrange("(cb p) o -> p cb o", p=128)
        wkv = wkT.rearrange("(cb p) o -> p cb o", p=128)

        v_evac = {}
        t_slab = {}
        for sl in range(4):
            par = sl % 2
            @block.gpsimd
            def _(g, sl=sl, par=par):
                if sl >= 2:
                    g.wait_ge(sm["t"].h, t_slab[sl - 2])
                sm["g"].inc(g.dma_start(out=WBUF[:, par, :, :], in_=wvv[:, :, sl * 256:(sl + 1) * 256]), 16)
            g_now = sm["g"].val
            for ti in range(NT):
                n = sl * NT + ti
                @block.tensor
                def _(t, sl=sl, par=par, ti=ti, n=n, g_now=g_now):
                    if ti == 0:
                        t.wait_ge(sm["g"].h, g_now)
                        if sl == 0:
                            t.wait_ge(sm["v"].h, tr1)
                    if n >= 2:
                        t.wait_ge(sm["v"].h, v_evac[n - 2])
                    pp = PS8[:, (n % 2) * 512:(n % 2) * 512 + 256]
                    ins = None
                    for cb in range(NCB):
                        ins = t.matmul(pp, LNT[:, cb, ti * 128:(ti + 1) * 128],
                                       WBUF[:, par, cb, :], start=(cb == 0), stop=(cb == NCB - 1))
                    sm["t"].inc(ins)
                t_mm = sm["t"].val
                if ti == NT - 1:
                    t_slab[sl] = t_mm
                @block.vector
                def _(v, sl=sl, ti=ti, n=n, t_mm=t_mm):
                    v.wait_ge(sm["t"].h, t_mm)
                    sm["v"].inc(v.tensor_copy(VB[:, ti, sl * 256:(sl + 1) * 256],
                                              PS8[:, (n % 2) * 512:(n % 2) * 512 + 256]))
                v_evac[n] = sm["v"].val
        v_done = sm["v"].val

        ctx_done = {}
        prev_ps_user = v_done
        qk_mm_done = {}
        sst_prev = [None]
        for hp in range(8):
            par_hp = hp % 2
            @block.gpsimd
            def _(g, hp=hp, par_hp=par_hp):
                if hp == 0:
                    g.wait_ge(sm["v"].h, v_done)
                else:
                    g.wait_ge(sm["t"].h, qk_mm_done[hp - 1])
                sm["g"].inc(g.dma_start(out=WBUF[:, par_hp, :, 0:128], in_=wqv[:, :, hp * 128:(hp + 1) * 128]), 16)
                sm["g"].inc(g.dma_start(out=WBUF[:, par_hp, :, 128:256], in_=wkv[:, :, hp * 128:(hp + 1) * 128]), 16)
            g_qk = sm["g"].val

            pu = prev_ps_user
            for m, DQK, FAC in ((0, QH, FQ), (1, KH, FK)):
                for tg in range(4):
                    @block.tensor
                    def _(t, m=m, tg=tg, g_qk=g_qk, par_hp=par_hp, pu=pu):
                        if m == 0 and tg == 0:
                            t.wait_ge(sm["g"].h, g_qk)
                            t.wait_ge(sm["v"].h, pu)
                        ins = None
                        for cb in range(NCB):
                            ins = t.matmul(PS8[:, tg * 512:(tg + 1) * 512],
                                           WBUF[:, par_hp, cb, m * 128:(m + 1) * 128],
                                           LNT[:, cb, tg * 512:(tg + 1) * 512],
                                           start=(cb == 0), stop=(cb == NCB - 1))
                        sm["t"].inc(ins)
                    t_mm = sm["t"].val
                    @block.vector
                    def _(v, m=m, tg=tg, t_mm=t_mm, DQK=DQK, FAC=FAC, hp=hp):
                        v.wait_ge(sm["t"].h, t_mm)
                        if m == 0 and tg == 0:
                            v.wait_ge(sm["g"].h, g_fqk)
                        dq3 = DQK[:, tg * 512:(tg + 1) * 512].rearrange("p (a b) -> p a b", b=128)
                        v.tensor_copy(dq3, PS8[:, tg * 512:(tg + 1) * 512].rearrange("p (a b) -> p a b", b=128))
                        sm["v"].inc(v.tensor_mul(
                            dq3, dq3, FAC[:, hp, :].unsqueeze(1).broadcast_to([128, 4, 128])))
            qk_mm_done[hp] = sm["t"].val
            qk_done = sm["v"].val

            for hh in range(2):
                h = hp * 2 + hh
                hs = hh * 64
                gam = GAM[h]
                for qt in range(NT):
                    nst = qt + 1
                    ngr = (nst + 3) // 4
                    v_sst = None
                    for gr in range(ngr):
                        s0 = gr * 4
                        s1 = min(nst, s0 + 4)
                        @block.tensor
                        def _(t, hh=hh, qt=qt, hs=hs, s0=s0, s1=s1, gr=gr, qk_done=qk_done):
                            if qt == 0 and hh == 0 and gr == 0:
                                t.wait_ge(sm["v"].h, qk_done)
                            if sst_prev[0] is not None:
                                t.wait_ge(sm["v"].h, sst_prev[0])
                            ins = None
                            for stt in range(s0, s1):
                                b0 = (stt - s0) * 512
                                ins = t.matmul(PS8[:, b0 + hh * 128:b0 + hh * 128 + 128],
                                               KH[hs:hs + 64, stt * 128:(stt + 1) * 128],
                                               QH[hs:hs + 64, qt * 128:(qt + 1) * 128],
                                               start=True, stop=True)
                            sm["t"].inc(ins)
                        t_sc = sm["t"].val

                        @block.vector
                        def _(v, qt=qt, hh=hh, s0=s0, s1=s1, gam=gam, t_sc=t_sc):
                            v.wait_ge(sm["t"].h, t_sc)
                            ins = None
                            for stt in range(s0, s1):
                                b0 = (stt - s0) * 512
                                pp = PS8[:, b0 + hh * 128:b0 + hh * 128 + 128]
                                dst = SST[:, stt * 128:(stt + 1) * 128]
                                if stt == qt:
                                    ins = v.tensor_mul(dst, pp, TRI[:, :])
                                else:
                                    c = gam ** (128 * (qt - stt))
                                    ins = v.tensor_scalar(dst, pp, float(c), None, ALU.mult)
                            sm["v"].inc(ins)
                        sst_prev[0] = sm["v"].val
                    v_sst = sm["v"].val

                    @block.tensor
                    def _(t, h=h, qt=qt, nst=nst, v_sst=v_sst):
                        t.wait_ge(sm["v"].h, v_sst)
                        if ctx_done:
                            t.wait_ge(sm["v"].h, max(ctx_done.values()))
                        ins = None
                        for stt in range(nst):
                            ins = t.matmul(PCTX[:, :], SST[:, stt * 128:(stt + 1) * 128],
                                           VB[:, stt, h * 64:(h + 1) * 64],
                                           start=(stt == 0), stop=(stt == nst - 1))
                        sm["t"].inc(ins)
                    t_ctx = sm["t"].val

                    @block.vector
                    def _(v, h=h, qt=qt, t_ctx=t_ctx):
                        v.wait_ge(sm["t"].h, t_ctx)
                        sm["v"].inc(v.tensor_copy(CTX[:, qt, h * 64:(h + 1) * 64], PCTX[:, :]))
                    ctx_done[(h, qt)] = sm["v"].val
            prev_ps_user = sm["v"].val
        attn_done = sm["v"].val

        @block.vector
        def _(v):
            v.wait_ge(sm["v"].h, attn_done)
            ins = None
            for ti in range(NT):
                ins = v.reduce_sum(GNP[:, 0, ti, :],
                                   CTX[:, ti, :].rearrange("p (h d) -> p h d", d=64), axis=AX.X)
            sm["v"].inc(ins)
        v_g1 = sm["v"].val

        sq_done = {}
        for ti in range(NT):
            @block.scalar
            def _(s, ti=ti):
                if ti == 0:
                    s.wait_ge(sm["v"].h, v_g1)
                if ti >= 1:
                    s.wait_ge(sm["v"].h, sq_done[ti - 1])
                sm["s"].inc(s.activation(scr[:, :], CTX[:, ti, :], AF.Square))
            s_sq = sm["s"].val
            @block.vector
            def _(v, ti=ti, s_sq=s_sq):
                v.wait_ge(sm["s"].h, s_sq)
                sm["v"].inc(v.reduce_sum(GNP[:, 1, ti, :],
                                         scr[:, :].rearrange("p (h d) -> p h d", d=64), axis=AX.X))
            sq_done[ti] = sm["v"].val
        gnp_done = sq_done[NT - 1]

        @block.tensor
        def _(t):
            t.wait_ge(sm["v"].h, gnp_done)
            sm["t"].inc(t.matmul(PROW[0:1, 0:512], st.cvec(1.0),
                                 GNP.rearrange("p a t h -> p (a t h)"),
                                 start=True, stop=True))
        t_red = sm["t"].val

        @block.vector
        def _(v):
            v.wait_ge(sm["t"].h, t_red)
            pv = PROW[0:1, 0:512].rearrange("p (a t h) -> p a h t", t=NT, h=16)
            sm["v"].inc(v.reduce_sum(
                GST[0:1, 0:32].rearrange("p (a h) -> p a h", a=2).unsqueeze(3),
                pv, axis=AX.X))
        v_red = sm["v"].val

        NTD = float(T * 64)
        @block.vector
        def _(v):
            v.wait_ge(sm["v"].h, v_red)
            g0 = GST[0:1, 0:32]
            mean = GST[0:1, 32:48]
            ex2 = GST[0:1, 48:64]
            var = GST[0:1, 64:80]
            v.tensor_copy(scr[0:1, 0:512], LNBF[0:1, 0, 0:512])
            v.tensor_scalar(GST[0:1, 32:64], g0, 1.0 / NTD, None, ALU.mult)
            v.tensor_copy(scr[0:1, 0:512], LNBF[0:1, 0, 0:512])
            v.tensor_mul(var, mean, mean)
            v.tensor_copy(scr[0:1, 0:512], LNBF[0:1, 0, 0:512])
            sm["v"].inc(v.tensor_sub(var, ex2, var))
        v_var = sm["v"].val

        @block.scalar
        def _(s):
            s.wait_ge(sm["v"].h, v_var)
            s.activation(GST[0:1, 80:96], GST[0:1, 64:80], AF.Ln, bias=st.cvec(1e-5)[0:1, :])
            s.activation(scr[0:1, 0:512], LNBF[0:1, 0, 0:512], AF.Square)
            sm["s"].inc(s.activation(GST[0:1, 80:96], GST[0:1, 80:96], AF.Exp, scale=-0.5))
        s_rstd = sm["s"].val

        @block.vector
        def _(v):
            v.wait_ge(sm["s"].h, s_rstd)
            rstd = GST[0:1, 80:96]
            mean = GST[0:1, 32:48]
            sa3 = GROW[0:1, 0, :].rearrange("p (h d) -> p h d", d=64)
            sb3 = GROW[0:1, 1, :].rearrange("p (h d) -> p h d", d=64)
            gnw3 = GROW[0:1, 2, :].rearrange("p (h d) -> p h d", d=64)
            gnb3 = GROW[0:1, 3, :].rearrange("p (h d) -> p h d", d=64)
            mrs = GST[0:1, 96:112]
            v.tensor_mul(mrs, mean, rstd)
            v.tensor_copy(scr[0:1, 0:512], LNBF[0:1, 0, 0:512])
            v.tensor_mul(sa3, gnw3, rstd.unsqueeze(2).broadcast_to([1, 16, 64]))
            v.tensor_mul(sb3, gnw3, mrs.unsqueeze(2).broadcast_to([1, 16, 64]))
            v.tensor_copy(scr[0:1, 0:512], LNBF[0:1, 0, 0:512])
            sm["v"].inc(v.tensor_sub(sb3, gnb3, sb3))
        v_sab = sm["v"].val

        bc_ev = None
        for r in range(2):
            for g_ in range(2):
                @block.tensor
                def _(t, r=r, g_=g_, bc_ev=bc_ev):
                    if r == 0 and g_ == 0:
                        t.wait_ge(sm["v"].h, v_sab)
                    else:
                        t.wait_ge(sm["v"].h, bc_ev)
                    t.wait_ge(sm["v"].h, v_ones)
                    sm["t"].inc(t.matmul(PROW[:, 0:512], ONESR[:, :],
                                         GROW[0:1, r, g_ * 512:(g_ + 1) * 512],
                                         start=True, stop=True))
                t_bc = sm["t"].val
                @block.vector
                def _(v, r=r, g_=g_, t_bc=t_bc):
                    v.wait_ge(sm["t"].h, t_bc)
                    sm["v"].inc(v.tensor_copy(SAB[:, r, g_ * 512:(g_ + 1) * 512], PROW[:, 0:512]))
                bc_ev = sm["v"].val
        v_bc = sm["v"].val

        @block.vector
        def _(v):
            v.wait_ge(sm["v"].h, v_bc)
            ins = None
            for ti in range(NT):
                v.tensor_mul(CTX[:, ti, :], CTX[:, ti, :], SAB[:, 0, :])
                ins = v.tensor_add(CTX[:, ti, :], CTX[:, ti, :], SAB[:, 1, :])
            sm["v"].inc(ins)
        gn_done = sm["v"].val

        _, tr2 = emit_tok2chan(block, sm, CTX, CTXT, IDENT, PTm, wait=[(sm["v"], gn_done)])

        @block.sync
        def _(sy):
            sy.wait_ge(sm["v"].h, attn_done)
            sm["d"].inc(sy.dma_start(out=X[:, :, :], in_=x.rearrange("(t p) c -> p t c", p=128)), 16)
        d_x2 = sm["d"].val

        yo_done = {}
        t_wsl = {}
        for sl in range(4):
            par = sl % 2
            @block.gpsimd
            def _(g, sl=sl, par=par):
                if sl == 0:
                    g.wait_ge(sm["v"].h, tr2)
                if sl >= 2:
                    g.wait_ge(sm["t"].h, t_wsl[sl - 2])
                sm["g"].inc(g.dma_start(out=WBUF[:, par, :, :], in_=wov[:, :, sl * 256:(sl + 1) * 256]), 16)
            g_now = sm["g"].val
            for ti in range(NT):
                n = sl * NT + ti
                @block.tensor
                def _(t, sl=sl, par=par, ti=ti, n=n, g_now=g_now):
                    if ti == 0:
                        t.wait_ge(sm["g"].h, g_now)
                        if sl == 0:
                            t.wait_ge(sm["v"].h, tr2)
                    if n >= 2:
                        t.wait_ge(sm["v"].h, yo_done[n - 2])
                    pp = PS8[:, (n % 2) * 512:(n % 2) * 512 + 256]
                    ins = None
                    for cb in range(NCB):
                        ins = t.matmul(pp, CTXT[:, cb, ti * 128:(ti + 1) * 128],
                                       WBUF[:, par, cb, :], start=(cb == 0), stop=(cb == NCB - 1))
                    sm["t"].inc(ins)
                t_mm = sm["t"].val
                if ti == NT - 1:
                    t_wsl[sl] = t_mm
                @block.vector
                def _(v, sl=sl, ti=ti, n=n, t_mm=t_mm):
                    v.wait_ge(sm["t"].h, t_mm)
                    if n == 0:
                        v.wait_ge(sm["d"].h, d_x2)
                    cs = slice(sl * 256, (sl + 1) * 256)
                    sm["v"].inc(v.tensor_add(X[:, ti, cs],
                                             PS8[:, (n % 2) * 512:(n % 2) * 512 + 256],
                                             X[:, ti, cs]))
                yo_done[n] = sm["v"].val
        fin = sm["v"].val

        @block.sync
        def _(sy):
            sy.wait_ge(sm["v"].h, fin)
            sm["d"].inc(sy.dma_start(out=y.rearrange("(t p) c -> p t c", p=128), in_=X[:, :, :]), 16)
        d_y = sm["d"].val

        @block.sync
        def _(sy):
            sy.wait_ge(sm["d"].h, d_y)

    def host_prep(b, x_shard, inputs):
        gam = np.array(GAM, np.float32)
        j = np.arange(128, dtype=np.float64)
        fqh = np.zeros((128, 8, 128), np.float32)
        fkh = np.zeros((128, 8, 128), np.float32)
        for hp in range(8):
            for half in range(2):
                g = float(gam[hp * 2 + half])
                fqh[half * 64:(half + 1) * 64, hp, :] = (0.125 * g ** j).astype(np.float32)[None, :]
                fkh[half * 64:(half + 1) * 64, hp, :] = (g ** (-j)).astype(np.float32)[None, :]
        tri = (np.arange(128)[None, :] >= np.arange(128)[:, None]).astype(np.float32)
        return {
            "x": x_shard,
            "wqT": np.ascontiguousarray(inputs["L2_attn_Wq"].T),
            "wkT": np.ascontiguousarray(inputs["L2_attn_Wk"].T),
            "wvT": np.ascontiguousarray(inputs["L2_attn_Wv"].T),
            "woT": np.ascontiguousarray(inputs["L2_attn_Wo"].T),
            "fq": fqh, "fk": fkh, "triu01": tri,
            "gnw": inputs["L2_attn_gn_w"][None, :],
            "gnb": inputs["L2_attn_gn_b"][None, :],
            "idn": IDENT128,
            "cvals": st.const_array(),
        }
    st.host_prep = host_prep
    return st


_BUILDERS["L2a"] = build_L2a



def build_L1a():
    st = Stage("L1a")
    x = st.dram_in("x", [T, C])
    wsT = st.dram_in("wsT", [C, C])
    wkeyT = st.dram_in("wkeyT", [C, 64])
    wvalT = st.dram_in("wvalT", [C, 64])
    wouT = st.dram_in("wouT", [64, C])
    sg = st.dram_in("sg", [1, C])
    lw_r = st.dram_in("lw_r", [128, C])
    lb_r = st.dram_in("lb_r", [128, C])
    td = st.dram_in("td", [64, 1])
    tf = st.dram_in("tf", [64, 1])
    idn = st.dram_in("idn", [128, 128], BF16)
    y = st.dram_out("y", [T, C])

    with st.es:
        st.alloc_consts()
        X = st.sbuf("X", [128, NT, C])
        XS = X
        LNBF = st.sbuf("LNBF", [128, NT, C], BF16)
        LNT = st.sbuf("LNT", [128, NCB, T], BF16)
        XNBF = LNT.rearrange("p cb t -> p (cb t)").rearrange("p (t c) -> p t c", c=C)
        XNT = LNBF.rearrange("p t c -> p (t c)").rearrange("p (cb t) -> p cb t", t=T)
        WBUF = st.sbuf("WBUF", [128, 2, 8, 256], BF16)
        WKV64 = st.sbuf("WKV64", [128, 2, 8, 64], BF16)
        WOU = st.sbuf("WOU", [64, C], BF16)
        GT = st.sbuf("GT", [128, C])
        GN1 = st.sbuf("GN1", [128, C], BF16)
        LW = st.sbuf("LW", [128, C])
        LB = st.sbuf("LB", [128, C])
        KS = st.sbuf("KS", [64, 2 * T])
        TDC = st.sbuf("TDC", [64, 4])
        GROW = st.sbuf("GROWa", [1, 2, C])
        ONESR = st.sbuf("ONESRa", [1, 128])
        IDENT = st.sbuf("IDENT", [128, 128], BF16)
        scr = st.sbuf("scr", [128, C])
        stats = st.sbuf("stats", [128, 5 * NT])
        PS8 = st.psum("ps8e", [128, 2048])
        PTm = [st.psum(f"ptmE{i}", [128, 128], BF16) for i in range(2)]
        PROW = st.psum("prowE", [128, 512])
        sm = st.sems()
        block = st.es.enter_context(st.nc.Block())
        st.load_consts(block, sm)

        KC = KS[:, 0:T]
        VC = KS[:, T:2 * T]
        WKVC = VC
        STC = KC
        SCB = KS.bitcast(BF16)[:, 2 * T:3 * T]

        @block.sync
        def _(sy):
            sm["d"].inc(sy.dma_start(out=X[:, :, :], in_=x.rearrange("(t p) c -> p t c", p=128)), 16)
            sm["d"].inc(sy.dma_start(out=IDENT[:, :], in_=idn[:, :]), 16)
            sm["d"].inc(sy.dma_start(out=GROW[0:1, 0, :], in_=sg[:, :]), 16)
            sm["d"].inc(sy.dma_start(out=LW[:, :], in_=lw_r[:, :]), 16)
            sm["d"].inc(sy.dma_start(out=LB[:, :], in_=lb_r[:, :]), 16)
            sm["d"].inc(sy.dma_start(out=TDC[:, 0:1], in_=td[:, :]), 16)
            sm["d"].inc(sy.dma_start(out=TDC[:, 2:3], in_=tf[:, :]), 16)
        d_all = sm["d"].val

        @block.gpsimd
        def _(g):
            sm["g"].inc(g.dma_start(out=WOU[:, :], in_=wouT[:, :]), 16)
        g_wou = sm["g"].val

        @block.vector
        def _(v):
            sm["v"].inc(v.memset(ONESR[:, :], 1.0))
        v_ones = sm["v"].val

        @block.scalar
        def _(s):
            s.wait_ge(sm["d"].h, d_all)
            s.activation(GROW[0:1, 1, :], GROW[0:1, 0, :], AF.Sigmoid)
            s.activation(TDC[:, 1:2], TDC[:, 0:1], AF.Exp)
            sm["s"].inc(s.activation(TDC[:, 3:4], TDC[:, 2:3], AF.Exp))
        s_gate = sm["s"].val

        @block.vector
        def _(v):
            v.wait_ge(sm["s"].h, s_gate)
            sm["v"].inc(v.tensor_scalar(TDC[:, 3:4], TDC[:, 3:4], -1.0, None, ALU.mult))
        v_netf = sm["v"].val

        bc_ev = None
        for g_ in range(2):
            @block.tensor
            def _(t, g_=g_, bc_ev=bc_ev):
                if g_ == 0:
                    t.wait_ge(sm["s"].h, s_gate)
                    t.wait_ge(sm["v"].h, v_ones)
                else:
                    t.wait_ge(sm["v"].h, bc_ev)
                sm["t"].inc(t.matmul(PROW[:, 0:512], ONESR[:, :],
                                     GROW[0:1, 1, g_ * 512:(g_ + 1) * 512],
                                     start=True, stop=True))
            t_bc = sm["t"].val
            @block.vector
            def _(v, g_=g_, t_bc=t_bc):
                v.wait_ge(sm["t"].h, t_bc)
                v.tensor_copy(GT[:, g_ * 512:(g_ + 1) * 512], PROW[:, 0:512])
                sm["v"].inc(v.tensor_scalar(GN1[:, g_ * 512:(g_ + 1) * 512], PROW[:, 0:512],
                                            -1.0, 1.0, ALU.mult, ALU.add))
            bc_ev = sm["v"].val
        v_gt = sm["v"].val

        _, ln_done = emit_ln(block, sm, X, LNBF, scr, stats, st.cvec(LN_EPS),
                             wait=[(sm["d"], d_all)])
        _, tr1 = emit_tok2chan(block, sm, LNBF, LNT, IDENT, PTm, wait=[(sm["v"], ln_done)])

        wsv = wsT.rearrange("(cb p) o -> p cb o", p=128)
        mx_done = {}
        t_wsl = {}
        for sl in range(4):
            par = sl % 2
            @block.gpsimd
            def _(g, sl=sl, par=par):
                if sl >= 2:
                    g.wait_ge(sm["t"].h, t_wsl[sl - 2])
                sm["g"].inc(g.dma_start(out=WBUF[:, par, :, :], in_=wsv[:, :, sl * 256:(sl + 1) * 256]), 16)
            g_now = sm["g"].val
            for ti in range(NT):
                n = sl * NT + ti
                tr_ti = (ti + 8) % 16
                @block.tensor
                def _(t, sl=sl, par=par, ti=ti, n=n, g_now=g_now, tr_ti=tr_ti):
                    if ti == 0:
                        t.wait_ge(sm["g"].h, g_now)
                        if sl == 0:
                            t.wait_ge(sm["v"].h, tr1)
                    if n >= 2:
                        t.wait_ge(sm["v"].h, mx_done[n - 2])
                    pp = PS8[:, (n % 2) * 512:(n % 2) * 512 + 256]
                    ins = None
                    for cb in range(NCB):
                        ins = t.matmul(pp, LNT[:, cb, tr_ti * 128:(tr_ti + 1) * 128],
                                       WBUF[:, par, cb, :], start=(cb == 0), stop=(cb == NCB - 1))
                    sm["t"].inc(ins)
                t_mm = sm["t"].val
                if ti == NT - 1:
                    t_wsl[sl] = t_mm
                @block.vector
                def _(v, sl=sl, ti=ti, n=n, t_mm=t_mm):
                    v.wait_ge(sm["t"].h, t_mm)
                    if n == 0:
                        v.wait_ge(sm["v"].h, v_gt)
                    cs = slice(sl * 256, (sl + 1) * 256)
                    pp = PS8[:, (n % 2) * 512:(n % 2) * 512 + 256]
                    v.tensor_mul(scr[:, 0:256], pp, GT[:, cs])
                    v.tensor_mul(scr[:, 256:512], LNBF[:, ti, cs], GN1[:, cs])
                    sm["v"].inc(v.tensor_add(XS[:, ti, cs], scr[:, 0:256], scr[:, 256:512]))
                mx_done[n] = sm["v"].val
        mix_done = sm["v"].val

        _, ln2_done = emit_ln(block, sm, XS, XNBF, scr, stats, st.cvec(LN_EPS),
                              wait=[(sm["v"], mix_done)], aff=(LW, LB))
        _, tr2 = emit_tok2chan(block, sm, XNBF, XNT, IDENT, PTm, wait=[(sm["v"], ln2_done)])

        @block.gpsimd
        def _(g):
            sm["g"].inc(g.dma_start(out=WKV64[:, 0, :, :], in_=wkeyT.rearrange("(cb p) s -> p cb s", p=128)), 16)
            sm["g"].inc(g.dma_start(out=WKV64[:, 1, :, :], in_=wvalT.rearrange("(cb p) s -> p cb s", p=128)), 16)
        g_kv = sm["g"].val
        WKV3 = WKV64

        kv_ev = {}
        for m in range(2):
            for tg in range(4):
                nn = m * 4 + tg
                @block.tensor
                def _(t, m=m, tg=tg, nn=nn):
                    if nn == 0:
                        t.wait_ge(sm["g"].h, g_kv)
                        t.wait_ge(sm["v"].h, tr2)
                        t.wait_ge(sm["v"].h, mix_done)
                    if nn >= 2:
                        t.wait_ge(sm["v"].h, kv_ev[nn - 2])
                    pp = PS8[:, (nn % 2) * 512:(nn % 2) * 512 + 512]
                    ins = None
                    for cb in range(NCB):
                        ins = t.matmul(pp[0:64, :], WKV3[:, m, cb, :],
                                       XNT[:, cb, tg * 512:(tg + 1) * 512],
                                       start=(cb == 0), stop=(cb == NCB - 1))
                    sm["t"].inc(ins)
                t_mm = sm["t"].val
                @block.vector
                def _(v, m=m, tg=tg, nn=nn, t_mm=t_mm):
                    v.wait_ge(sm["t"].h, t_mm)
                    dst = KC if m == 0 else VC
                    sm["v"].inc(v.tensor_copy(dst[:, tg * 512:(tg + 1) * 512],
                                              PS8[0:64, (nn % 2) * 512:(nn % 2) * 512 + 512]))
                kv_ev[nn] = sm["v"].val
        kv_done = sm["v"].val

        @block.scalar
        def _(s):
            s.wait_ge(sm["v"].h, kv_done)
            sm["s"].inc(s.activation(KC[:, :], KC[:, :], AF.Sigmoid))
        s_sig = sm["s"].val

        @block.vector
        def _(v):
            v.wait_ge(sm["s"].h, s_sig)
            v.wait_ge(sm["v"].h, v_netf)
            sm["v"].inc(v.tensor_scalar(KC[:, :], KC[:, :], TDC[:, 3:4], None, ALU.mult))
        v_km = sm["v"].val

        @block.scalar
        def _(s):
            s.wait_ge(sm["v"].h, v_km)
            sm["s"].inc(s.activation(KC[:, :], KC[:, :], AF.Exp))
        s_ek = sm["s"].val

        @block.vector
        def _(v):
            v.wait_ge(sm["s"].h, s_ek)
            v.tensor_mul(WKVC[:, :], KC[:, :], VC[:, :])
            v.tensor_copy(scr[0:64, 0:512], GT[0:64, 0:512])
            v.tensor_tensor_scan(STC[:, :], TDC[:, 1:2].broadcast_to([64, T]),
                                 WKVC[:, :], 0.0, ALU.mult, ALU.add)
            v.tensor_copy(scr[0:64, 0:512], GT[0:64, 0:512])
            sm["v"].inc(v.tensor_copy(SCB[:, :], STC[:, :]))
        v_scan = sm["v"].val

        @block.sync
        def _(sy):
            sy.wait_ge(sm["v"].h, ln2_done)
            sm["d"].inc(sy.dma_start(out=X[:, :, :], in_=x.rearrange("(t p) c -> p t c", p=128)), 16)
        d_x2 = sm["d"].val

        out_ev = {}
        for ti in range(NT):
            @block.tensor
            def _(t, ti=ti):
                if ti == 0:
                    t.wait_ge(sm["v"].h, v_scan)
                    t.wait_ge(sm["g"].h, g_wou)
                if ti >= 2:
                    t.wait_ge(sm["v"].h, out_ev[ti - 2])
                pp = PS8[:, (ti % 2) * 1024:(ti % 2) * 1024 + 1024]
                ins = None
                for f in range(2):
                    ins = t.matmul(pp[:, f * 512:(f + 1) * 512],
                                   SCB[:, ti * 128:(ti + 1) * 128],
                                   WOU[:, f * 512:(f + 1) * 512],
                                   start=True, stop=True)
                sm["t"].inc(ins)
            t_mm = sm["t"].val
            @block.vector
            def _(v, ti=ti, t_mm=t_mm):
                v.wait_ge(sm["t"].h, t_mm)
                if ti == 0:
                    v.wait_ge(sm["d"].h, d_x2)
                pp = PS8[:, (ti % 2) * 1024:(ti % 2) * 1024 + 1024]
                sm["v"].inc(v.tensor_add(X[:, ti, :], pp, X[:, ti, :]))
            out_ev[ti] = sm["v"].val
        fin = sm["v"].val

        @block.sync
        def _(sy):
            sy.wait_ge(sm["v"].h, fin)
            sm["d"].inc(sy.dma_start(out=y.rearrange("(t p) c -> p t c", p=128), in_=X[:, :, :]), 16)
        d_y = sm["d"].val

        @block.sync
        def _(sy):
            sy.wait_ge(sm["d"].h, d_y)

    def host_prep(b, x_shard, inputs):
        return {
            "x": x_shard,
            "wsT": np.ascontiguousarray(inputs["L1_attn_W_shift"].T),
            "wkeyT": np.ascontiguousarray(inputs["L1_attn_W_key"].T),
            "wvalT": np.ascontiguousarray(inputs["L1_attn_W_value"].T),
            "wouT": np.ascontiguousarray(inputs["L1_attn_W_output"].T),
            "sg": inputs["L1_attn_shift_gate"][None, :],
            "lw_r": rep128(inputs["L1_attn_ln_w"]),
            "lb_r": rep128(inputs["L1_attn_ln_b"]),
            "td": np.ascontiguousarray(inputs["L1_attn_time_decay"][:, None]),
            "tf": np.ascontiguousarray(inputs["L1_attn_time_first"][:, None]),
            "idn": IDENT128,
            "cvals": st.const_array(),
        }
    st.host_prep = host_prep
    return st


_BUILDERS["L1a"] = build_L1a



def _emit_bitonic_desc(v, flat, base, span, TMPS):
    for k in [2, 4, 8, 16, 32, 64]:
        j = k // 2
        while j >= 1:
            if k < 64:
                nb = span // (2 * k)
                nm = k // (2 * j)
                full = flat[:, base:base + span].rearrange(
                    "p (b d m e r) -> p b d m e r", b=nb, d=2, m=nm, e=2)
                A0 = full[:, :, 0, :, 0, :]
                B0 = full[:, :, 0, :, 1, :]
                A1 = full[:, :, 1, :, 0, :]
                B1 = full[:, :, 1, :, 1, :]
                half = span // 4
                T0 = TMPS[:, 0:half].rearrange("p (b m r) -> p b m r", b=nb, m=nm)
                T1 = TMPS[:, half:2 * half].rearrange("p (b m r) -> p b m r", b=nb, m=nm)
                v.tensor_tensor(T0, A0, B0, ALU.min)
                v.tensor_tensor(T1, A1, B1, ALU.min)
                v.tensor_tensor(A0, A0, B0, ALU.max)
                v.tensor_tensor(B1, A1, B1, ALU.max)
                v.tensor_copy(B0, T0)
                v.tensor_copy(A1, T1)
            else:
                nb = span // 64
                nm = 64 // (2 * j)
                full = flat[:, base:base + span].rearrange(
                    "p (b m e r) -> p b m e r", b=nb, m=nm, e=2)
                A0 = full[:, :, :, 0, :]
                B0 = full[:, :, :, 1, :]
                half = span // 2
                T0 = TMPS[:, 0:half].rearrange("p (b m r) -> p b m r", b=nb, m=nm)
                v.tensor_tensor(T0, A0, B0, ALU.min)
                v.tensor_tensor(A0, A0, B0, ALU.max)
                v.tensor_copy(B0, T0)
            j //= 2


def build_L3a():
    st = Stage("L3a")
    x = st.dram_in("x", [T, C])
    watt = st.dram_in("watt", [C, 3 * C])
    batt_r = st.dram_in("batt_r", [128, 3 * C])
    wproj = st.dram_in("wproj", [C, C])
    bproj_r = st.dram_in("bproj_r", [128, C])
    kvr = st.dram_in("kvr", [128, 1024])
    idn = st.dram_in("idn", [128, 128], BF16)
    trilm = st.dram_in("trilm", [128, 128])
    y = st.dram_out("y", [T, C])

    with st.es:
        st.alloc_consts()
        KF = st.sbuf("KF", [128, NT, C])
        VBF = st.sbuf("VBF", [128, NT, C], BF16)
        LNBF = st.sbuf("LNBF", [128, NT, C], BF16)
        LNT = st.sbuf("LNT", [128, NCB, T], BF16)
        QB = LNBF
        KSORT = LNT.rearrange("p cb t -> p (cb t)").rearrange("p (t c) -> p t c", c=C)
        TMPS = st.sbuf("TMPS", [128, 4096], BF16)
        WBUF = st.sbuf("WBUF", [128, 2, 8, 256], BF16)
        BATT = st.sbuf("BATT", [128, 3 * C])
        BPR = BATT[:, 0:C]
        KV = BATT[:, 2 * C:3 * C]
        PSB = st.sbuf("PSB", [128, T], BF16)
        PTT = st.sbuf("PTT", [128, T], BF16)
        TRIL = st.sbuf("TRIL", [128, 128])
        IDENT = st.sbuf("IDENT", [128, 128], BF16)
        scr = st.sbuf("scr", [128, C])
        scr2 = st.sbuf("scr2", [128, C])
        scr3 = BATT[:, C:2 * C]
        scrb = st.sbuf("scrb", [128, C], BF16)
        RH = st.sbuf("RH", [128, 64])
        rowst = st.sbuf("rowst", [128, 8])
        stats = st.sbuf("stats", [128, 5 * NT])
        PS8 = st.psum("ps8f", [128, 2048])
        PTm = [st.psum(f"ptmF{i}", [128, 128], BF16) for i in range(2)]
        PCTX = st.psum("pctxF", [128, 64])
        sm = st.sems()
        block = st.es.enter_context(st.nc.Block())
        st.load_consts(block, sm)

        X = KF
        KFB = KF.rearrange("p t c -> p (t c)").bitcast(BF16)
        QT = KFB[:, 0:16384].rearrange("p (cb t) -> p cb t", t=T)
        PT = KFB[:, 16384:32768].rearrange("p (cb t) -> p cb t", t=T)
        CTX = KSORT
        CTXT = QB.rearrange("p t c -> p (t c)").rearrange("p (cb t) -> p cb t", t=T)
        KS_flat = KSORT.rearrange("p t c -> p (t c)")

        @block.sync
        def _(sy):
            sm["d"].inc(sy.dma_start(out=X[:, :, :], in_=x.rearrange("(t p) c -> p t c", p=128)), 16)
            sm["d"].inc(sy.dma_start(out=IDENT[:, :], in_=idn[:, :]), 16)
            sm["d"].inc(sy.dma_start(out=TRIL[:, :], in_=trilm[:, :]), 16)
            sm["d"].inc(sy.dma_start(out=BATT[:, :], in_=batt_r[:, :]), 16)
        d_all = sm["d"].val

        @block.vector
        def _(v):
            v.wait_ge(sm["d"].h, d_all)
            sm["v"].inc(v.tensor_scalar(BATT[:, 0:C], BATT[:, 0:C], 0.125, None, ALU.mult))
        v_bq = sm["v"].val

        _, ln_done = emit_ln(block, sm, X, LNBF, scr, stats, st.cvec(LN_EPS),
                             wait=[(sm["d"], d_all)])
        _, tr1 = emit_tok2chan(block, sm, LNBF, LNT, IDENT, PTm, wait=[(sm["v"], ln_done)])

        wav = watt.rearrange("(cb p) o -> p cb o", p=128)
        qkv_ev = {}
        t_wsl = {}
        for sl in range(12):
            par = sl % 2
            @block.gpsimd
            def _(g, sl=sl, par=par):
                if sl >= 2:
                    g.wait_ge(sm["t"].h, t_wsl[sl - 2])
                sm["g"].inc(g.dma_start(out=WBUF[:, par, :, :], in_=wav[:, :, sl * 256:(sl + 1) * 256]), 16)
            g_now = sm["g"].val
            for ti in range(NT):
                n = sl * NT + ti
                @block.tensor
                def _(t, sl=sl, par=par, ti=ti, n=n, g_now=g_now):
                    if ti == 0:
                        t.wait_ge(sm["g"].h, g_now)
                        if sl == 0:
                            t.wait_ge(sm["v"].h, tr1)
                    if n >= 2:
                        t.wait_ge(sm["v"].h, qkv_ev[n - 2])
                    pp = PS8[:, (n % 2) * 512:(n % 2) * 512 + 256]
                    ins = None
                    for cb in range(NCB):
                        ins = t.matmul(pp, LNT[:, cb, ti * 128:(ti + 1) * 128],
                                       WBUF[:, par, cb, :], start=(cb == 0), stop=(cb == NCB - 1))
                    sm["t"].inc(ins)
                t_mm = sm["t"].val
                if ti == NT - 1:
                    t_wsl[sl] = t_mm
                @block.vector
                def _(v, sl=sl, ti=ti, n=n, t_mm=t_mm):
                    v.wait_ge(sm["t"].h, t_mm)
                    if n == 0:
                        v.wait_ge(sm["v"].h, v_bq)
                    pp = PS8[:, (n % 2) * 512:(n % 2) * 512 + 256]
                    o0 = sl * 256
                    if sl < 4:
                        ins = v.scalar_tensor_tensor(QB[:, ti, o0:o0 + 256], pp, 0.125,
                                                     BATT[:, o0:o0 + 256], ALU.mult, ALU.add)
                    elif sl < 8:
                        ins = v.tensor_add(KF[:, ti, o0 - C:o0 - C + 256], pp,
                                           BATT[:, o0:o0 + 256])
                    else:
                        ins = v.tensor_add(VBF[:, ti, o0 - 2 * C:o0 - 2 * C + 256], pp,
                                           BATT[:, o0:o0 + 256])
                    sm["v"].inc(ins)
                qkv_ev[n] = sm["v"].val
        qkv_done = sm["v"].val

        @block.sync
        def _(sy):
            sy.wait_ge(sm["v"].h, qkv_done)
            sm["d"].inc(sy.dma_start(out=BPR[:, :], in_=bproj_r[:, :]), 16)
            sm["d"].inc(sy.dma_start(out=KV[:, :], in_=kvr[:, :]), 16)
        d_bk = sm["d"].val

        @block.vector
        def _(v):
            v.wait_ge(sm["v"].h, qkv_done)
            v.tensor_copy(KSORT[:, :, :], KF[:, :, :])
            v.tensor_copy(scr[:, 0:512], KF[:, 0, 0:512])
            for half in range(2):
                _emit_bitonic_desc(v, KS_flat, half * 8192, 8192, TMPS)
            sm["v"].inc(v.tensor_copy(scr[:, 0:512], KF[:, 0, 0:512]))
        v_sort = sm["v"].val

        one_ap = st.cvec(1.0)
        p_done = {}
        for ti in range(NT):
            @block.vector
            def _(v, ti=ti):
                if ti == 0:
                    v.wait_ge(sm["v"].h, v_sort)
                    v.wait_ge(sm["d"].h, d_bk)
                v.tensor_copy(scr2[:, :], KSORT[:, ti, :])
                v.tensor_copy(scr[:, 0:512], KF[:, 0, 0:512])
                for h in range(16):
                    v.tensor_tensor_scan(scr[:, h * 64:(h + 1) * 64],
                                         one_ap.broadcast_to([128, 64]),
                                         scr2[:, h * 64:(h + 1) * 64], 0.0,
                                         ALU.mult, ALU.add)
                v.tensor_mul(scr3[:, :], scr2[:, :], KV[:, :])
                v.tensor_scalar(scr[:, :], scr[:, :], -1.0, None, ALU.add)
                v.tensor_tensor(scr3[:, :], scr3[:, :], scr[:, :], ALU.is_gt)
                v.tensor_copy(scr[:, 0:512], KF[:, 0, 0:512])
                v.reduce_sum(RH[:, 0:16], scr3.rearrange("p (h d) -> p h d", d=64), axis=AX.X)
                v.tensor_mul(scr3[:, :], scr3[:, :], scr2[:, :])
                v.tensor_copy(scr[:, 0:512], KF[:, 0, 0:512])
                v.reduce_sum(RH[:, 16:32], scr3.rearrange("p (h d) -> p h d", d=64), axis=AX.X)
                v.tensor_scalar(RH[:, 32:48], RH[:, 0:16], 1.0, None, ALU.max)
                v.tensor_scalar(RH[:, 48:64], RH[:, 16:32], -1.0, None, ALU.add)
                v.reciprocal(RH[:, 32:48], RH[:, 32:48])
                v.tensor_copy(scr[:, 0:512], KF[:, 0, 0:512])
                v.tensor_mul(RH[:, 48:64], RH[:, 48:64], RH[:, 32:48])
                v.tensor_copy(scr[:, 0:512], KF[:, 0, 0:512])
                v.tensor_sub(scr[:, :].rearrange("p (h d) -> p h d", d=64),
                             KF[:, ti, :].rearrange("p (h d) -> p h d", d=64),
                             RH[:, 48:64].unsqueeze(2).broadcast_to([128, 16, 64]))
                v.tensor_copy(scr2[:, 0:512], KF[:, 0, 0:512])
                sm["v"].inc(v.tensor_scalar(KSORT[:, ti, :], scr[:, :], 0.0, None, ALU.max))
            p_done[ti] = sm["v"].val
        p_all = p_done[NT - 1]

        _, tr_q = emit_tok2chan(block, sm, QB, QT, IDENT, PTm, wait=[(sm["v"], p_all)])
        _, tr_p = emit_tok2chan(block, sm, KSORT, PT, IDENT, PTm, wait=[(sm["v"], tr_q)])

        ctx_done = {}
        exp_done_prev = [None]
        for h in range(16):
            hp = h // 2
            hh = h % 2
            hs = (h % 2) * 64
            hb = h // 2
            for qt in range(NT):
                L = (qt + 1) * 128
                @block.tensor
                def _(t, h=h, qt=qt, L=L):
                    if h == 0 and qt == 0:
                        t.wait_ge(sm["v"].h, tr_p)
                    if exp_done_prev[0] is not None:
                        t.wait_ge(sm["s"].h, exp_done_prev[0])
                    ins = None
                    for g_ in range((L + 511) // 512):
                        f0 = g_ * 512
                        f1 = min(L, f0 + 512)
                        ins = t.matmul(PS8[:, f0:f1],
                                       QT[(h % 2) * 64:(h % 2) * 64 + 64, h // 2, qt * 128:(qt + 1) * 128],
                                       PT[(h % 2) * 64:(h % 2) * 64 + 64, h // 2, f0:f1],
                                       start=True, stop=True)
                    sm["t"].inc(ins)
                t_sc = sm["t"].val

                @block.vector
                def _(v, qt=qt, L=L, t_sc=t_sc):
                    v.wait_ge(sm["t"].h, t_sc)
                    v.tensor_add(PS8[:, qt * 128:(qt + 1) * 128],
                                 PS8[:, qt * 128:(qt + 1) * 128], TRIL[:, :])
                    v.tensor_copy(scr[:, 0:512], BATT[:, 0:512])
                    v.reduce_max(rowst[:, 0:1], PS8[:, 0:L], axis=AX.X)
                    v.tensor_copy(scr[:, 0:512], BATT[:, 0:512])
                    sm["v"].inc(v.tensor_scalar(rowst[:, 1:2], rowst[:, 0:1], -1.0, None, ALU.mult))
                v_m = sm["v"].val

                @block.scalar
                def _(s, L=L, v_m=v_m):
                    s.wait_ge(sm["v"].h, v_m)
                    sm["s"].inc(s.activation(PSB[:, 0:L], PS8[:, 0:L], AF.Exp, bias=rowst[:, 1:2]))
                exp_done_prev[0] = sm["s"].val
                s_exp = sm["s"].val

                @block.vector
                def _(v, L=L, s_exp=s_exp):
                    v.wait_ge(sm["s"].h, s_exp)
                    v.reduce_sum(rowst[:, 2:3], PSB[:, 0:L], axis=AX.X)
                    v.tensor_copy(scr[:, 0:512], BATT[:, 0:512])
                    sm["v"].inc(v.reciprocal(rowst[:, 3:4], rowst[:, 2:3]))
                v_rs = sm["v"].val

                nst = qt + 1
                base_vv = sm["v"].val
                tr_ev = {n: base_vv + n + 1 for n in range(nst)}
                @block.tensor
                def _(t, nst=nst, s_exp=s_exp, tr_ev=tr_ev):
                    t.wait_ge(sm["s"].h, s_exp)
                    for n in range(nst):
                        if n >= 2:
                            t.wait_ge(sm["v"].h, tr_ev[n - 2])
                        sm["t"].inc(t.transpose(PTm[n % 2][:, :],
                                                PSB[:, n * 128:(n + 1) * 128], IDENT[:, :]))
                t_tr0 = sm["t"].val - nst

                @block.vector
                def _(v, nst=nst, t_tr0=t_tr0):
                    for n in range(nst):
                        v.wait_ge(sm["t"].h, t_tr0 + n + 1)
                        sm["v"].inc(v.tensor_copy(PTT[:, n * 128:(n + 1) * 128], PTm[n % 2][:, :]))

                @block.tensor
                def _(t, nst=nst, h=h, qt=qt, tr_ev=tr_ev):
                    if ctx_done:
                        t.wait_ge(sm["v"].h, max(ctx_done.values()))
                    ins = None
                    for n in range(nst):
                        t.wait_ge(sm["v"].h, tr_ev[n])
                        ins = t.matmul(PCTX[:, :], PTT[:, n * 128:(n + 1) * 128],
                                       VBF[:, n, h * 64:(h + 1) * 64],
                                       start=(n == 0), stop=(n == nst - 1))
                    sm["t"].inc(ins)
                t_ctx = sm["t"].val

                @block.vector
                def _(v, h=h, qt=qt, t_ctx=t_ctx, v_rs=v_rs):
                    v.wait_ge(sm["t"].h, t_ctx)
                    v.wait_ge(sm["v"].h, v_rs)
                    sm["v"].inc(v.tensor_scalar(CTX[:, qt, h * 64:(h + 1) * 64],
                                                PCTX[:, :], rowst[:, 3:4], None, ALU.mult))
                ctx_done[(h, qt)] = sm["v"].val
        attn_done = sm["v"].val

        rej_done = {}
        for ti in range(NT):
            @block.scalar
            def _(s, ti=ti):
                if ti == 0:
                    s.wait_ge(sm["v"].h, attn_done)
                if ti >= 1:
                    s.wait_ge(sm["v"].h, rej_done[ti - 1])
                sm["s"].inc(s.activation(scr[:, :], VBF[:, ti, :], AF.Square))
            s_sq = sm["s"].val
            @block.vector
            def _(v, ti=ti, s_sq=s_sq):
                v.wait_ge(sm["s"].h, s_sq)
                v.reduce_sum(RH[:, 0:16], scr.rearrange("p (h d) -> p h d", d=64), axis=AX.X)
                v.tensor_mul(scr2[:, :], CTX[:, ti, :], VBF[:, ti, :])
                v.reduce_sum(RH[:, 16:32], scr2.rearrange("p (h d) -> p h d", d=64), axis=AX.X)
                v.tensor_scalar(RH[:, 0:16], RH[:, 0:16], 1e-24, None, ALU.max)
                v.tensor_copy(scr[:, 0:512], BATT[:, 0:512])
                v.reciprocal(RH[:, 32:48], RH[:, 0:16])
                v.tensor_copy(scr[:, 0:512], BATT[:, 0:512])
                v.tensor_mul(RH[:, 48:64], RH[:, 16:32], RH[:, 32:48])
                v.tensor_copy(scr[:, 0:512], BATT[:, 0:512])
                v.tensor_copy(scrb[:, 0:16], RH[:, 48:64])
                v.tensor_copy(scr[:, 0:512], BATT[:, 0:512])
                v.tensor_mul(scrb.rearrange("p (h d) -> p h d", d=64),
                             VBF[:, ti, :].rearrange("p (h d) -> p h d", d=64),
                             scrb[:, 0:16].unsqueeze(2).broadcast_to([128, 16, 64]))
                v.tensor_copy(scr[:, 0:512], BATT[:, 0:512])
                sm["v"].inc(v.tensor_sub(CTX[:, ti, :], CTX[:, ti, :], scrb[:, :]))
            rej_done[ti] = sm["v"].val
        rej_all = rej_done[NT - 1]

        _, tr2 = emit_tok2chan(block, sm, CTX, CTXT, IDENT, PTm, wait=[(sm["v"], rej_all)])

        @block.sync
        def _(sy):
            sy.wait_ge(sm["v"].h, attn_done)
            sm["d"].inc(sy.dma_start(out=KF[:, :, :], in_=x.rearrange("(t p) c -> p t c", p=128)), 16)
        d_x2 = sm["d"].val

        wpv = wproj.rearrange("(cb p) o -> p cb o", p=128)
        yo_done = {}
        t_wsl2 = {}
        for sl in range(4):
            par = sl % 2
            @block.gpsimd
            def _(g, sl=sl, par=par):
                if sl == 0:
                    g.wait_ge(sm["v"].h, tr2)
                if sl >= 2:
                    g.wait_ge(sm["t"].h, t_wsl2[sl - 2])
                sm["g"].inc(g.dma_start(out=WBUF[:, par, :, :], in_=wpv[:, :, sl * 256:(sl + 1) * 256]), 16)
            g_now = sm["g"].val
            for ti in range(NT):
                n = sl * NT + ti
                @block.tensor
                def _(t, sl=sl, par=par, ti=ti, n=n, g_now=g_now):
                    if ti == 0:
                        t.wait_ge(sm["g"].h, g_now)
                        if sl == 0:
                            t.wait_ge(sm["v"].h, tr2)
                    if n >= 2:
                        t.wait_ge(sm["v"].h, yo_done[n - 2])
                    pp = PS8[:, (n % 2) * 512:(n % 2) * 512 + 256]
                    ins = None
                    for cb in range(NCB):
                        ins = t.matmul(pp, CTXT[:, cb, ti * 128:(ti + 1) * 128],
                                       WBUF[:, par, cb, :], start=(cb == 0), stop=(cb == NCB - 1))
                    sm["t"].inc(ins)
                t_mm = sm["t"].val
                if ti == NT - 1:
                    t_wsl2[sl] = t_mm
                @block.vector
                def _(v, sl=sl, ti=ti, n=n, t_mm=t_mm):
                    v.wait_ge(sm["t"].h, t_mm)
                    if n == 0:
                        v.wait_ge(sm["d"].h, d_x2)
                    cs = slice(sl * 256, (sl + 1) * 256)
                    pp = PS8[:, (n % 2) * 512:(n % 2) * 512 + 256]
                    v.tensor_add(scr[:, 0:256], pp, BPR[:, cs])
                    sm["v"].inc(v.tensor_add(KF[:, ti, cs], scr[:, 0:256], KF[:, ti, cs]))
                yo_done[n] = sm["v"].val
        fin = sm["v"].val

        @block.sync
        def _(sy):
            sy.wait_ge(sm["v"].h, fin)
            sm["d"].inc(sy.dma_start(out=y.rearrange("(t p) c -> p t c", p=128), in_=KF[:, :, :]), 16)
        d_y = sm["d"].val

        @block.sync
        def _(sy):
            sy.wait_ge(sm["d"].h, d_y)

    def host_prep(b, x_shard, inputs):
        tril = np.where(np.arange(128)[None, :] <= np.arange(128)[:, None], 0.0, NEG_BIG).astype(np.float32)
        kv = np.tile(np.arange(1, 65, dtype=np.float32), 16)
        return {
            "x": x_shard,
            "watt": np.ascontiguousarray(inputs["L3_attn_W_attn"]),
            "batt_r": rep128(inputs["L3_attn_b_attn"]),
            "wproj": np.ascontiguousarray(inputs["L3_attn_W_proj"]),
            "bproj_r": rep128(inputs["L3_attn_b_proj"]),
            "kvr": rep128(kv),
            "idn": IDENT128,
            "trilm": tril,
            "cvals": st.const_array(),
        }
    st.host_prep = host_prep
    return st


_BUILDERS["L3a"] = build_L3a



SCAN_T = 512


def build_L0a_scan():
    st = Stage("L0sc")
    kc_d = st.dram_in("kc", [SCAN_T, C])
    vc_d = st.dram_in("vc", [SCAN_T, C])
    rc_d = st.dram_in("rc", [SCAN_T, C])
    s_in = st.dram_in("s_in", [128, 512])
    decs = st.dram_in("decs", [16])
    etas = st.dram_in("etas", [16])
    hsel = st.dram_in("hsel", [128, 2])
    hselT = st.dram_in("hselT", [2, 128])
    idf = st.dram_in("idf", [128, 128])
    s_out = st.dram_out("s_out", [128, 512])
    wkv_d = st.dram_out("wkv", [SCAN_T, C])

    NTS = SCAN_T // 128

    with st.es:
        st.alloc_consts()
        KC = st.sbuf("KC", [128, NTS, C])
        RC = st.sbuf("RC", [128, NTS, C])
        VC = st.sbuf("VC", [128, NTS, C])
        VS = st.sbuf("VS", [128, 8, SCAN_T])
        WKC = st.sbuf("WKC", [128, 8, SCAN_T])
        WT = st.sbuf("WT", [128, NTS, C])
        S = st.sbuf("S", [128, 512])
        SCRB = st.sbuf("SCRB", [128, 512])
        PRED = st.sbuf("PRED", [128, 8 * 4])
        PART = st.sbuf("PART", [128, 8])
        ETD = st.sbuf("ETD", [128, 8])
        CS = st.sbuf("CS", [2, 64])
        JNKV = st.sbuf("JNKV", [2, 8])
        JNKS = st.sbuf("JNKS", [2, 8])
        HS = st.sbuf("HS", [128, 2])
        HST = st.sbuf("HST", [2, 128])
        IDF = st.sbuf("IDF", [128, 128])
        PKB = [st.psum(f"pkb{i}", [128, 512]) for i in range(2)]
        PRB = [st.psum(f"prb{i}", [128, 512]) for i in range(2)]
        PN = st.psum("pn", [2, 8])
        PF = st.psum("pf", [128, 8])
        PTm = [st.psum(f"ptmG{i}", [128, 128]) for i in range(2)]
        sm = st.sems()
        block = st.es.enter_context(st.nc.Block())
        st.load_consts(block, sm)

        DEC28 = CS[0:2, 0:8]
        ET28 = CS[0:2, 8:16]
        ETD28 = CS[0:2, 16:24]
        N2M = CS[0:2, 24:32]
        RINV = CS[0:2, 32:40]
        FDEC = CS[0:2, 40:48]
        LNO = CS[0:2, 48:56]
        FTMP = CS[0:2, 56:64]

        @block.sync
        def _(sy):
            sm["d"].inc(sy.dma_start(out=KC[:, :, :], in_=kc_d.rearrange("(t p) c -> p t c", p=128)), 16)
            sm["d"].inc(sy.dma_start(out=RC[:, :, :], in_=rc_d.rearrange("(t p) c -> p t c", p=128)), 16)
            sm["d"].inc(sy.dma_start(out=VC[:, :, :], in_=vc_d.rearrange("(t p) c -> p t c", p=128)), 16)
            sm["d"].inc(sy.dma_start(out=S[:, :], in_=s_in[:, :]), 16)
            sm["d"].inc(sy.dma_start(out=HS[:, :], in_=hsel[:, :]), 16)
            sm["d"].inc(sy.dma_start(out=HST[:, :], in_=hselT[:, :]), 16)
            sm["d"].inc(sy.dma_start(out=IDF[:, :], in_=idf[:, :]), 16)
            sm["d"].inc(sy.dma_start(out=DEC28, in_=decs.rearrange("(hq hh) -> hh hq", hh=2), allow_slow_non_contiguous=True), 16)
            sm["d"].inc(sy.dma_start(out=ET28, in_=etas.rearrange("(hq hh) -> hh hq", hh=2), allow_slow_non_contiguous=True), 16)
        d_all = sm["d"].val

        @block.vector
        def _(v):
            v.wait_ge(sm["d"].h, d_all)
            v.reciprocal(JNKV[:, :], DEC28)
            v.tensor_copy(JNKS[:, :], DEC28)
            sm["v"].inc(v.tensor_mul(ETD28, JNKV[:, :], ET28))
        v_etd = sm["v"].val

        @block.tensor
        def _(t):
            t.wait_ge(sm["v"].h, v_etd)
            sm["t"].inc(t.matmul(PF[:, 0:8], HST[:, :], ETD28, start=True, stop=True))
        t_etd = sm["t"].val

        @block.vector
        def _(v):
            v.wait_ge(sm["t"].h, t_etd)
            sm["v"].inc(v.tensor_copy(ETD[:, :], PF[:, 0:8]))
        v_etdr = sm["v"].val

        t0 = sm["t"].val
        v0 = sm["v"].val
        @block.tensor
        def _(t):
            t.wait_ge(sm["d"].h, d_all)
            k = 0
            for h in range(16):
                for ti in range(NTS):
                    if k >= 2:
                        t.wait_ge(sm["v"].h, v0 + k - 1)
                    sm["t"].inc(t.transpose(PTm[k % 2][0:64, :],
                                            VC[:, ti, h * 64:(h + 1) * 64], IDF[:, :]))
                    k += 1
        @block.vector
        def _(v):
            k = 0
            for h in range(16):
                hh, hq = h % 2, h // 2
                for ti in range(NTS):
                    v.wait_ge(sm["t"].h, t0 + k + 1)
                    sm["v"].inc(v.tensor_copy(VS[hh * 64:hh * 64 + 64, hq, ti * 128:(ti + 1) * 128],
                                              PTm[k % 2][0:64, :]))
                    k += 1
        v_vs = sm["v"].val

        T0 = sm["t"].val
        V0 = sm["v"].val
        S0 = sm["s"].val
        t_bc = {tt: T0 + 3 * tt + 1 for tt in range(SCAN_T)}
        t_norm = {tt: T0 + 3 * tt + 2 for tt in range(SCAN_T)}
        t_fbc = {tt: T0 + 3 * tt + 3 for tt in range(SCAN_T)}
        v_sqred = {tt: V0 + 4 * tt + 1 for tt in range(SCAN_T)}
        v_n2m = {tt: V0 + 4 * tt + 2 for tt in range(SCAN_T)}
        v_fch = {tt: V0 + 4 * tt + 3 for tt in range(SCAN_T)}
        v_outer = {tt: V0 + 4 * tt + 4 for tt in range(SCAN_T)}
        s_ln = {tt: S0 + tt + 1 for tt in range(SCAN_T)}

        @block.tensor
        def _(t):
            t.wait_ge(sm["v"].h, v_vs)
            t.wait_ge(sm["v"].h, v_etdr)
            for tt in range(SCAN_T):
                par = tt % 2
                ti, pr = tt // 128, tt % 128
                col = IDF[:, pr:pr + 1]
                if tt >= 2:
                    t.wait_ge(sm["v"].h, v_outer[tt - 2])
                kview = KC[:, ti, :].rearrange("p (h j) -> p h j", j=64)
                rview = RC[:, ti, :].rearrange("p (h j) -> p h j", j=64)
                t.matmul(PKB[par][0:64, :], col.broadcast_to([128, 64]),
                         kview[:, 0::2, :], start=True, stop=True)
                t.matmul(PKB[par][64:128, :], col.broadcast_to([128, 64]),
                         kview[:, 1::2, :], start=True, stop=True)
                t.matmul(PRB[par][0:64, :], col.broadcast_to([128, 64]),
                         rview[:, 0::2, :], start=True, stop=True)
                sm["t"].inc(t.matmul(PRB[par][64:128, :], col.broadcast_to([128, 64]),
                                     rview[:, 1::2, :], start=True, stop=True))
                t.wait_ge(sm["v"].h, v_sqred[tt])
                sm["t"].inc(t.matmul(PN[:, :], HS[:, :], PART[:, :], start=True, stop=True))
                t.wait_ge(sm["v"].h, v_fch[tt])
                sm["t"].inc(t.matmul(PF[:, :], HST[:, :], FDEC, start=True, stop=True))
        assert sm["t"].val == T0 + 3 * SCAN_T

        @block.scalar
        def _(s):
            for tt in range(SCAN_T):
                s.wait_ge(sm["v"].h, v_n2m[tt])
                s.activation(LNO, N2M, AF.Ln)
                s.activation(JNKS[:, :], DEC28, AF.Copy)
                sm["s"].inc(s.activation(RINV, LNO, AF.Exp, scale=-0.5))
        assert sm["s"].val == S0 + SCAN_T

        @block.vector
        def _(v):
            S3 = S.rearrange("p (h j) -> p h j", j=64)
            SCR3 = SCRB.rearrange("p (h j) -> p h j", j=64)
            for tt in range(SCAN_T):
                par = tt % 2
                v.wait_ge(sm["t"].h, t_bc[tt])
                v.tensor_mul(SCR3, S3, PKB[par].rearrange("p (h j) -> p h j", j=64))
                v.reduce_sum(PRED[:, 0:8], SCR3, axis=AX.X)
                v.tensor_copy(PRED[:, 24:32], ETD[:, :])
                v.tensor_sub(PRED[:, 8:16], PRED[:, 0:8], VS[:, :, tt])
                v.tensor_copy(PRED[:, 24:32], ETD[:, :])
                v.tensor_mul(PRED[:, 16:24], PRED[:, 8:16], ETD[:, :])
                v.tensor_copy(PRED[:, 24:32], ETD[:, :])
                v.tensor_tensor(SCR3, PKB[par].rearrange("p (h j) -> p h j", j=64),
                                PRED[:, 16:24].unsqueeze(2).broadcast_to([128, 8, 64]),
                                ALU.mult)
                v.tensor_sub(S[:, :], S[:, :], SCRB[:, :])
                v.tensor_mul(SCRB[:, :], S[:, :], S[:, :])
                sm["v"].inc(v.reduce_sum(PART[:, :].unsqueeze(2), SCR3, axis=AX.X))
                v.wait_ge(sm["t"].h, t_norm[tt])
                sm["v"].inc(v.tensor_scalar(N2M, PN[:, :], 1e-16, None, ALU.max))
                v.wait_ge(sm["s"].h, s_ln[tt])
                v.tensor_scalar(FTMP, RINV, 16.0, 1.0, ALU.mult, ALU.min)
                v.tensor_copy(JNKV[:, :], DEC28)
                sm["v"].inc(v.tensor_mul(FDEC, FTMP, DEC28))
                v.wait_ge(sm["t"].h, t_fbc[tt])
                v.tensor_tensor(S3, S3,
                                PF[:, :].unsqueeze(2).broadcast_to([128, 8, 64]),
                                ALU.mult)
                v.tensor_mul(SCR3, S3, PRB[par].rearrange("p (h j) -> p h j", j=64))
                sm["v"].inc(v.reduce_sum(WKC[:, :, tt].unsqueeze(2), SCR3, axis=AX.X))
        assert sm["v"].val == V0 + 4 * SCAN_T
        scan_done = sm["v"].val

        t1 = sm["t"].val
        v1 = sm["v"].val
        @block.tensor
        def _(t):
            t.wait_ge(sm["v"].h, scan_done)
            k = 0
            for h in range(16):
                hh, hq = h % 2, h // 2
                for ti in range(NTS):
                    if k >= 2:
                        t.wait_ge(sm["v"].h, v1 + k - 1)
                    sm["t"].inc(t.transpose(PTm[k % 2][:, 0:64],
                                            WKC[hh * 64:hh * 64 + 64, hq, ti * 128:(ti + 1) * 128],
                                            IDF[hh * 64:hh * 64 + 64, hh * 64:hh * 64 + 64]))
                    k += 1
        @block.vector
        def _(v):
            k = 0
            for h in range(16):
                for ti in range(NTS):
                    v.wait_ge(sm["t"].h, t1 + k + 1)
                    sm["v"].inc(v.tensor_copy(WT[:, ti, h * 64:(h + 1) * 64], PTm[k % 2][:, 0:64]))
                    k += 1
        v_wt = sm["v"].val

        @block.sync
        def _(sy):
            sy.wait_ge(sm["v"].h, v_wt)
            sm["d"].inc(sy.dma_start(out=wkv_d.rearrange("(t p) c -> p t c", p=128), in_=WT[:, :, :]), 16)
            sm["d"].inc(sy.dma_start(out=s_out[:, :], in_=S[:, :]), 16)
        d_out = sm["d"].val

        @block.sync
        def _(sy):
            sy.wait_ge(sm["d"].h, d_out)

    st.host_prep = None
    return st


_BUILDERS["L0a_scan"] = build_L0a_scan


def build_L0a_pre():
    st = Stage("L0pre")
    x = st.dram_in("x", [T, C])
    wrT = st.dram_in("wrT", [C, C])
    wkT = st.dram_in("wkT", [C, C])
    wvT = st.dram_in("wvT", [C, C])
    wgT = st.dram_in("wgT", [C, C])
    br = st.dram_in("br", [128, C])
    bk = st.dram_in("bk", [128, C])
    bv = st.dram_in("bv", [128, C])
    bg = st.dram_in("bg", [128, C])
    tmc = st.dram_in("tmc", [128, 4, 8])
    decay = st.dram_in("decay", [16, 1])
    eta = st.dram_in("eta", [16, 1])
    idf = st.dram_in("idf", [128, 128])
    rsig_d = st.dram_out("rsig", [T, C])
    khat_d = st.dram_out("khat", [T, C])
    vhat_d = st.dram_out("vhat", [T, C])
    gsil_d = st.dram_out("gsil", [T, C])
    decs_d = st.dram_out("decs", [16, 1])
    etas_d = st.dram_out("etas", [16, 1])

    with st.es:
        st.alloc_consts()
        X = st.sbuf("X", [128, NT, C])
        LNTF = st.sbuf("LNTF", [128, NCB, T])
        XMT = st.sbuf("XMT", [128, NCB, T], BF16)
        WBUF = st.sbuf("WBUF", [128, 2, 8, 256], BF16)
        BIAS = st.sbuf("BIAS", [128, C])
        TMC = st.sbuf("TMC", [128, 4, 8])
        DE = st.sbuf("DE", [16, 2])
        RIN = st.sbuf("RIN", [128, 16])
        IDF = st.sbuf("IDF", [128, 128])
        scr = st.sbuf("scr", [128, 2048])
        stats = st.sbuf("stats", [128, 5 * NT])
        PP = [st.psum(f"ppG{i}", [128, 512]) for i in range(2)]
        PTm = [st.psum(f"ptmH{i}", [128, 128]) for i in range(2)]
        sm = st.sems()
        block = st.es.enter_context(st.nc.Block())
        st.load_consts(block, sm)
        STG = X

        @block.sync
        def _(sy):
            sm["d"].inc(sy.dma_start(out=X[:, :, :], in_=x.rearrange("(t p) c -> p t c", p=128)), 16)
            sm["d"].inc(sy.dma_start(out=IDF[:, :], in_=idf[:, :]), 16)
            sm["d"].inc(sy.dma_start(out=TMC[:, :, :], in_=tmc[:, :, :]), 16)
            sm["d"].inc(sy.dma_start(out=DE[:, 0:1], in_=decay[:, :]), 16)
            sm["d"].inc(sy.dma_start(out=DE[:, 1:2], in_=eta[:, :]), 16)
        d_all = sm["d"].val

        @block.scalar
        def _(s):
            s.wait_ge(sm["d"].h, d_all)
            sm["s"].inc(s.activation(DE[:, :], DE[:, :], AF.Sigmoid))
        s_de = sm["s"].val

        @block.sync
        def _(sy):
            sy.wait_ge(sm["s"].h, s_de)
            sm["d"].inc(sy.dma_start(out=decs_d[:, :], in_=DE[:, 0:1]), 16)
            sm["d"].inc(sy.dma_start(out=etas_d[:, :], in_=DE[:, 1:2]), 16)
        d_de = sm["d"].val

        _, ln_done = emit_ln(block, sm, X, X, scr[:, 0:C], stats, st.cvec(LN_EPS),
                             wait=[(sm["d"], d_all)])
        t0 = sm["t"].val
        v0 = sm["v"].val
        @block.tensor
        def _(t):
            t.wait_ge(sm["v"].h, ln_done)
            k = 0
            for cb in range(NCB):
                for ti in range(NT):
                    if k >= 2:
                        t.wait_ge(sm["v"].h, v0 + k - 1)
                    sm["t"].inc(t.transpose(PTm[k % 2][:, :],
                                            X[:, ti, cb * 128:(cb + 1) * 128], IDF[:, :]))
                    k += 1
        @block.vector
        def _(v):
            k = 0
            for cb in range(NCB):
                for ti in range(NT):
                    v.wait_ge(sm["t"].h, t0 + k + 1)
                    sm["v"].inc(v.tensor_copy(LNTF[:, cb, ti * 128:(ti + 1) * 128], PTm[k % 2][:, :]))
                    k += 1
        tr_done = sm["v"].val

        wviews = {
            0: (wrT.rearrange("(cb p) o -> p cb o", p=128), br, rsig_d, "sig"),
            1: (wkT.rearrange("(cb p) o -> p cb o", p=128), bk, khat_d, "l2"),
            2: (wvT.rearrange("(cb p) o -> p cb o", p=128), bv, vhat_d, "l2"),
            3: (wgT.rearrange("(cb p) o -> p cb o", p=128), bg, gsil_d, "silu"),
        }

        prev_m_done = tr_done
        t_slab_hist = {}
        for m in range(4):
            wv_, bias_d, out_d, kind = wviews[m]
            @block.vector
            def _(v, m=m, prev_m_done=prev_m_done):
                v.wait_ge(sm["v"].h, prev_m_done)
                ins = None
                for cb in range(NCB):
                    d_ = scr[:, 0:T]
                    v.tensor_sub(d_[:, 1:T], LNTF[:, cb, 1:T], LNTF[:, cb, 0:T - 1])
                    v.tensor_copy(d_[:, 0:1], LNTF[:, cb, 0:1])
                    v.tensor_scalar(d_, d_, TMC[:, m, cb:cb + 1], None, ALU.mult)
                    v.tensor_add(XMT[:, cb, 1:T], LNTF[:, cb, 0:T - 1], d_[:, 1:T])
                    ins = v.tensor_copy(XMT[:, cb, 0:1], d_[:, 0:1])
                sm["v"].inc(ins)
            v_mix = sm["v"].val

            @block.sync
            def _(sy, m=m, bias_d=bias_d, prev_m_done=prev_m_done):
                sy.wait_ge(sm["v"].h, prev_m_done)
                sm["d"].inc(sy.dma_start(out=BIAS[:, :], in_=bias_d[:, :]), 16)
            d_b = sm["d"].val

            ev = {}
            t_sl = {}
            for sl in range(4):
                par = sl % 2
                @block.gpsimd
                def _(g, m=m, sl=sl, par=par, wv_=wv_):
                    if m > 0 or sl >= 2:
                        if (m * 4 + sl) >= 2:
                            g.wait_ge(sm["t"].h, t_slab_hist[m * 4 + sl - 2])
                    sm["g"].inc(g.dma_start(out=WBUF[:, par, :, :],
                                            in_=wv_[:, :, sl * 256:(sl + 1) * 256]), 16)
                g_now = sm["g"].val
                for ti in range(NT):
                    n = sl * NT + ti
                    @block.tensor
                    def _(t, m=m, sl=sl, par=par, ti=ti, n=n, g_now=g_now, v_mix=v_mix):
                        if ti == 0:
                            t.wait_ge(sm["g"].h, g_now)
                            if sl == 0:
                                t.wait_ge(sm["v"].h, v_mix)
                        if n >= 2:
                            t.wait_ge(sm["v"].h, ev[n - 2])
                        pp = PP[n % 2]
                        ins = None
                        for cb in range(NCB):
                            ins = t.matmul(pp[:, 0:256], XMT[:, cb, ti * 128:(ti + 1) * 128],
                                           WBUF[:, par, cb, :], start=(cb == 0), stop=(cb == NCB - 1))
                        sm["t"].inc(ins)
                    t_mm = sm["t"].val
                    if ti == NT - 1:
                        t_sl[sl] = t_mm
                        t_slab_hist[m * 4 + sl] = t_mm

                    if kind == "sig" or kind == "silu":
                        @block.vector
                        def _(v, sl=sl, ti=ti, n=n, t_mm=t_mm, d_b=d_b):
                            v.wait_ge(sm["t"].h, t_mm)
                            if n == 0:
                                v.wait_ge(sm["d"].h, d_b)
                            cs = slice(sl * 256, (sl + 1) * 256)
                            sm["v"].inc(v.tensor_add(scr[:, 1024:1280], PP[n % 2][:, 0:256], BIAS[:, cs]))
                        v_b = sm["v"].val
                        @block.scalar
                        def _(s, sl=sl, ti=ti, v_b=v_b, kind=kind):
                            s.wait_ge(sm["v"].h, v_b)
                            cs = slice(sl * 256, (sl + 1) * 256)
                            fn = AF.Sigmoid if kind == "sig" else AF.Silu
                            sm["s"].inc(s.activation(STG[:, ti, cs], scr[:, 1024:1280], fn))
                        ev[n] = None
                        s_now = sm["s"].val
                        @block.vector
                        def _(v, s_now=s_now):
                            v.wait_ge(sm["s"].h, s_now)
                            sm["v"].inc(v.tensor_copy(scr[:, 1280:1288], BIAS[:, 0:8]))
                        ev[n] = sm["v"].val
                    else:
                        @block.vector
                        def _(v, sl=sl, ti=ti, n=n, t_mm=t_mm, d_b=d_b):
                            v.wait_ge(sm["t"].h, t_mm)
                            if n == 0:
                                v.wait_ge(sm["d"].h, d_b)
                            cs = slice(sl * 256, (sl + 1) * 256)
                            sm["v"].inc(v.tensor_add(STG[:, ti, cs], PP[n % 2][:, 0:256], BIAS[:, cs]))
                        ev[n] = sm["v"].val
            proj_done = ev[4 * NT - 1]

            if kind == "l2":
                st_done = {}
                for ti in range(NT):
                    @block.vector
                    def _(v, ti=ti, proj_done=proj_done):
                        if ti == 0:
                            v.wait_ge(sm["v"].h, proj_done)
                        k3 = STG[:, ti, :].rearrange("p (h dd) -> p h dd", dd=64)
                        v.tensor_mul(scr[:, 0:1024], STG[:, ti, :], STG[:, ti, :])
                        v.reduce_sum(RIN[:, 0:16].unsqueeze(2),
                                     scr[:, 0:1024].rearrange("p (h dd) -> p h dd", dd=64), axis=AX.X)
                        v.tensor_copy(scr[:, 1280:1288], BIAS[:, 0:8])
                        sm["v"].inc(v.tensor_scalar(RIN[:, 0:16], RIN[:, 0:16], 1e-24, None, ALU.max))
                    v_n = sm["v"].val
                    @block.scalar
                    def _(s, ti=ti, v_n=v_n):
                        s.wait_ge(sm["v"].h, v_n)
                        s.activation(RIN[:, 16:32] if False else RIN[:, 0:16], RIN[:, 0:16], AF.Ln)
                        s.activation(scr[:, 1288:1296], BIAS[:, 0:8], AF.Copy)
                        sm["s"].inc(s.activation(RIN[:, 0:16], RIN[:, 0:16], AF.Exp, scale=-0.5))
                    s_n = sm["s"].val
                    @block.vector
                    def _(v, ti=ti, s_n=s_n):
                        v.wait_ge(sm["s"].h, s_n)
                        sm["v"].inc(v.tensor_tensor(
                            STG[:, ti, :].rearrange("p (h dd) -> p h dd", dd=64),
                            STG[:, ti, :].rearrange("p (h dd) -> p h dd", dd=64),
                            RIN[:, 0:16].unsqueeze(2).broadcast_to([128, 16, 64]), ALU.mult))
                    v_nm = sm["v"].val
                    @block.sync
                    def _(sy, ti=ti, v_nm=v_nm, out_d=out_d):
                        sy.wait_ge(sm["v"].h, v_nm)
                        sm["d"].inc(sy.dma_start(
                            out=out_d.rearrange("(t p) c -> p t c", p=128)[:, ti, :],
                            in_=STG[:, ti, :]), 16)
                    st_done[ti] = sm["d"].val
                store_done = st_done[NT - 1]
            else:
                @block.sync
                def _(sy, proj_done=proj_done, out_d=out_d):
                    sy.wait_ge(sm["v"].h, proj_done)
                    sm["d"].inc(sy.dma_start(out=out_d.rearrange("(t p) c -> p t c", p=128),
                                             in_=STG[:, :, :]), 16)
                store_done = sm["d"].val
            @block.vector
            def _(v, store_done=store_done):
                v.wait_ge(sm["d"].h, store_done)
                sm["v"].inc(v.tensor_copy(scr[:, 1280:1288], BIAS[:, 0:8]))
            prev_m_done = sm["v"].val

        @block.sync
        def _(sy):
            sy.wait_ge(sm["v"].h, prev_m_done)
            sy.wait_ge(sm["d"].h, d_de)

    def host_prep(b, x_shard, inputs):
        tmm = np.zeros((128, 4, 8), np.float32)
        for i, nm in enumerate(["time_mix_r", "time_mix_k", "time_mix_v", "time_mix_g"]):
            tmm[:, i, :] = inputs[f"L0_attn_{nm}"].reshape(8, 128).T
        return {
            "x": x_shard,
            "wrT": np.ascontiguousarray(inputs["L0_attn_W_r"].T),
            "wkT": np.ascontiguousarray(inputs["L0_attn_W_k"].T),
            "wvT": np.ascontiguousarray(inputs["L0_attn_W_v"].T),
            "wgT": np.ascontiguousarray(inputs["L0_attn_W_g"].T),
            "br": rep128(inputs["L0_attn_b_r"]),
            "bk": rep128(inputs["L0_attn_b_k"]),
            "bv": rep128(inputs["L0_attn_b_v"]),
            "bg": rep128(inputs["L0_attn_b_g"]),
            "tmc": tmm,
            "decay": np.ascontiguousarray(inputs["L0_attn_decay"].reshape(16, 1)),
            "eta": np.ascontiguousarray(inputs["L0_attn_eta"].reshape(16, 1)),
            "idf": np.eye(128, dtype=np.float32),
            "cvals": st.const_array(),
        }
    st.host_prep = host_prep
    return st


def build_L0a_post(stop=3):
    st = Stage("L0post")
    x = st.dram_in("x", [T, C])
    wkv_d = st.dram_in("wkv", [T, C])
    gsil_d = st.dram_in("gsil", [T, C])
    gnw = st.dram_in("gnw", [1, C])
    gnb = st.dram_in("gnb", [1, C])
    woT = st.dram_in("woT", [C, C])
    bo_r = st.dram_in("bo_r", [128, C])
    idn = st.dram_in("idn", [128, 128], BF16)
    y = st.dram_out("y", [T, C])

    with st.es:
        st.alloc_consts()
        X = st.sbuf("X", [128, NT, C])
        WK = st.sbuf("WK", [128, NT, C])
        WNT = st.sbuf("WNT", [128, NCB, T], BF16)
        ZNS = st.sbuf("ZNS", [128, 2, C], BF16)
        WBUF = st.sbuf("WBUF", [128, 2, 8, 256], BF16)
        GSB = st.sbuf("GSB", [128, 2, C])
        GNP = st.sbuf("GNP", [128, 2, NT, 16])
        GROW = st.sbuf("GROW", [1, 2, C])
        GST = st.sbuf("GST", [1, 128])
        SAB = st.sbuf("SAB", [128, 2, C])
        BO = st.sbuf("BO", [128, C])
        ONESR = st.sbuf("ONESR", [1, 128])
        IDENT = st.sbuf("IDENT", [128, 128], BF16)
        scr = st.sbuf("scr", [128, C])
        PS8 = st.psum("ps8h", [128, 2048])
        PTm = [st.psum(f"ptmI{i}", [128, 128], BF16) for i in range(2)]
        PROW = st.psum("prowH", [128, 512])
        sm = st.sems()
        block = st.es.enter_context(st.nc.Block())
        st.load_consts(block, sm)

        @block.sync
        def _(sy):
            sm["d"].inc(sy.dma_start(out=X[:, :, :], in_=x.rearrange("(t p) c -> p t c", p=128)), 16)
            sm["d"].inc(sy.dma_start(out=WK[:, :, :], in_=wkv_d.rearrange("(t p) c -> p t c", p=128)), 16)
            sm["d"].inc(sy.dma_start(out=IDENT[:, :], in_=idn[:, :]), 16)
            sm["d"].inc(sy.dma_start(out=GSB[0:1, 0, :], in_=gnw[:, :]), 16)
            sm["d"].inc(sy.dma_start(out=GSB[0:1, 1, :], in_=gnb[:, :]), 16)
            sm["d"].inc(sy.dma_start(out=BO[:, :], in_=bo_r[:, :]), 16)
        d_all = sm["d"].val

        @block.vector
        def _(v):
            sm["v"].inc(v.memset(ONESR[:, :], 1.0))
        v_ones = sm["v"].val

        @block.vector
        def _(v):
            v.wait_ge(sm["d"].h, d_all)
            ins = None
            for ti in range(NT):
                ins = v.reduce_sum(GNP[:, 0, ti, :],
                                   WK[:, ti, :].rearrange("p (h d) -> p h d", d=64), axis=AX.X)
            sm["v"].inc(ins)
        v_g1 = sm["v"].val

        sq_done = {}
        for ti in range(NT):
            @block.scalar
            def _(s, ti=ti):
                if ti == 0:
                    s.wait_ge(sm["v"].h, v_g1)
                if ti >= 1:
                    s.wait_ge(sm["v"].h, sq_done[ti - 1])
                sm["s"].inc(s.activation(scr[:, :], WK[:, ti, :], AF.Square))
            s_sq = sm["s"].val
            @block.vector
            def _(v, ti=ti, s_sq=s_sq):
                v.wait_ge(sm["s"].h, s_sq)
                sm["v"].inc(v.reduce_sum(GNP[:, 1, ti, :],
                                         scr[:, :].rearrange("p (h d) -> p h d", d=64), axis=AX.X))
            sq_done[ti] = sm["v"].val
        gnp_done = sq_done[NT - 1]

        @block.tensor
        def _(t):
            t.wait_ge(sm["v"].h, gnp_done)
            t.wait_ge(sm["d"].h, d_all)
            sm["t"].inc(t.matmul(PROW[0:1, 0:512], st.cvec(1.0),
                                 GNP.rearrange("p a t h -> p (a t h)"), start=True, stop=True))
        t_red = sm["t"].val

        @block.vector
        def _(v):
            v.wait_ge(sm["t"].h, t_red)
            pv = PROW[0:1, 0:512].rearrange("p (a t h) -> p a h t", t=NT, h=16)
            sm["v"].inc(v.reduce_sum(GST[0:1, 0:32].rearrange("p (a h) -> p a h", a=2).unsqueeze(3),
                                     pv, axis=AX.X))
        v_red = sm["v"].val

        NTD = float(T * 64)
        @block.vector
        def _(v):
            v.wait_ge(sm["v"].h, v_red)
            g0 = GST[0:1, 0:32]
            mean = GST[0:1, 32:48]
            ex2 = GST[0:1, 48:64]
            var = GST[0:1, 64:80]
            v.tensor_copy(scr[0:1, 512:576], WK[0:1, 0, 0:64])
            v.tensor_scalar(GST[0:1, 32:64], g0, 1.0 / NTD, None, ALU.mult)
            v.tensor_copy(scr[0:1, 576:640], WK[0:1, 0, 0:64])
            v.tensor_mul(var, mean, mean)
            v.tensor_copy(scr[0:1, 640:704], WK[0:1, 0, 0:64])
            sm["v"].inc(v.tensor_sub(var, ex2, var))
        v_var = sm["v"].val

        @block.scalar
        def _(s):
            s.wait_ge(sm["v"].h, v_var)
            s.activation(GST[0:1, 80:96], GST[0:1, 64:80], AF.Ln, bias=st.cvec(1e-5)[0:1, :])
            s.activation(scr[0:1, 0:512], WK[0:1, 0, 0:512], AF.Square)
            sm["s"].inc(s.activation(GST[0:1, 80:96], GST[0:1, 80:96], AF.Exp, scale=-0.5))
        s_rstd = sm["s"].val

        @block.vector
        def _(v):
            v.wait_ge(sm["s"].h, s_rstd)
            rstd = GST[0:1, 80:96]
            mean = GST[0:1, 32:48]
            sa3 = GROW[0:1, 0, :].rearrange("p (h d) -> p h d", d=64)
            sb3 = GROW[0:1, 1, :].rearrange("p (h d) -> p h d", d=64)
            gnw3 = GSB[0:1, 0, :].rearrange("p (h d) -> p h d", d=64)
            gnb3 = GSB[0:1, 1, :].rearrange("p (h d) -> p h d", d=64)
            mrs = GST[0:1, 96:112]
            v.tensor_mul(mrs, mean, rstd)
            v.tensor_copy(scr[0:1, 704:768], WK[0:1, 0, 0:64])
            v.tensor_mul(sa3, gnw3, rstd.unsqueeze(2).broadcast_to([1, 16, 64]))
            v.tensor_mul(sb3, gnw3, mrs.unsqueeze(2).broadcast_to([1, 16, 64]))
            v.tensor_copy(scr[0:1, 768:832], WK[0:1, 0, 0:64])
            sm["v"].inc(v.tensor_sub(sb3, gnb3, sb3))
        v_sab = sm["v"].val

        bc_ev = None
        for r in range(2):
            for g_ in range(2):
                @block.tensor
                def _(t, r=r, g_=g_, bc_ev=bc_ev):
                    if r == 0 and g_ == 0:
                        t.wait_ge(sm["v"].h, v_sab)
                        t.wait_ge(sm["v"].h, v_ones)
                    else:
                        t.wait_ge(sm["v"].h, bc_ev)
                    sm["t"].inc(t.matmul(PROW[:, 0:512], ONESR[:, :],
                                         GROW[0:1, r, g_ * 512:(g_ + 1) * 512], start=True, stop=True))
                t_bc = sm["t"].val
                @block.vector
                def _(v, r=r, g_=g_, t_bc=t_bc):
                    v.wait_ge(sm["t"].h, t_bc)
                    sm["v"].inc(v.tensor_copy(SAB[:, r, g_ * 512:(g_ + 1) * 512], PROW[:, 0:512]))
                bc_ev = sm["v"].val
        v_bc = sm["v"].val
        if stop <= 1:
            @block.sync
            def _(sy):
                sy.wait_ge(sm["v"].h, v_bc)
                sm["d"].inc(sy.dma_start(out=y.rearrange("(t p) c -> p t c", p=128), in_=X[:, :, :]), 16)
            dv = sm["d"].val
            @block.sync
            def _(sy):
                sy.wait_ge(sm["d"].h, dv)
            st.host_prep = _mk_post_prep(st)
            return st

        zn_done = {}
        trv = {}
        T0z = sm["t"].val
        V0z = sm["v"].val
        for ti in range(NT):
            zn_done[ti] = V0z + 9 * ti + 1
            for cb in range(NCB):
                trv[(ti, cb)] = V0z + 9 * ti + 1 + cb + 1

        @block.vector
        def _(v):
            v.wait_ge(sm["v"].h, v_bc)
            for ti in range(NT):
                v.tensor_mul(scr[:, :], WK[:, ti, :], SAB[:, 0, :])
                sm["v"].inc(v.tensor_add(ZNS[:, ti % 2, :], scr[:, :], SAB[:, 1, :]))
                for cb in range(NCB):
                    k = ti * NCB + cb
                    v.wait_ge(sm["t"].h, T0z + k + 1)
                    sm["v"].inc(v.tensor_copy(WNT[:, cb, ti * 128:(ti + 1) * 128],
                                              PTm[k % 2][:, :]))

        @block.tensor
        def _(t):
            for ti in range(NT):
                t.wait_ge(sm["v"].h, zn_done[ti])
                for cb in range(NCB):
                    k = ti * NCB + cb
                    if k >= 2:
                        t.wait_ge(sm["v"].h, trv[((k - 2) // NCB, (k - 2) % NCB)])
                    sm["t"].inc(t.transpose(PTm[k % 2][:, :],
                                            ZNS[:, ti % 2, cb * 128:(cb + 1) * 128],
                                            IDENT[:, :]))
        tr2 = sm["v"].val
        if stop <= 2:
            @block.sync
            def _(sy):
                sy.wait_ge(sm["v"].h, tr2)
                sm["d"].inc(sy.dma_start(out=y.rearrange("(t p) c -> p t c", p=128), in_=X[:, :, :]), 16)
            dv = sm["d"].val
            @block.sync
            def _(sy):
                sy.wait_ge(sm["d"].h, dv)
            st.host_prep = _mk_post_prep(st)
            return st

        wov = woT.rearrange("(cb p) o -> p cb o", p=128)
        yo_done = {}
        t_wsl = {}
        for sl in range(4):
            par = sl % 2
            @block.gpsimd
            def _(g, sl=sl, par=par):
                if sl == 0:
                    g.wait_ge(sm["v"].h, tr2)
                if sl >= 2:
                    g.wait_ge(sm["t"].h, t_wsl[sl - 2])
                sm["g"].inc(g.dma_start(out=WBUF[:, par, :, :], in_=wov[:, :, sl * 256:(sl + 1) * 256]), 16)
            g_now = sm["g"].val
            for ti in range(NT):
                n = sl * NT + ti
                @block.tensor
                def _(t, sl=sl, par=par, ti=ti, n=n, g_now=g_now):
                    if ti == 0:
                        t.wait_ge(sm["g"].h, g_now)
                        if sl == 0:
                            t.wait_ge(sm["v"].h, tr2)
                    if n >= 2:
                        t.wait_ge(sm["v"].h, yo_done[n - 2])
                    pp = PS8[:, (n % 2) * 512:(n % 2) * 512 + 256]
                    ins = None
                    for cb in range(NCB):
                        ins = t.matmul(pp, WNT[:, cb, ti * 128:(ti + 1) * 128],
                                       WBUF[:, par, cb, :], start=(cb == 0), stop=(cb == NCB - 1))
                    sm["t"].inc(ins)
                t_mm = sm["t"].val
                if ti == NT - 1:
                    t_wsl[sl] = t_mm
                @block.vector
                def _(v, sl=sl, ti=ti, n=n, t_mm=t_mm):
                    v.wait_ge(sm["t"].h, t_mm)
                    cs = slice(sl * 256, (sl + 1) * 256)
                    pp = PS8[:, (n % 2) * 512:(n % 2) * 512 + 256]
                    sm["v"].inc(v.tensor_add(WK[:, ti, cs], pp, BO[:, cs]))
                yo_done[n] = sm["v"].val
        wo_all = sm["v"].val

        gfin = {}
        V0f = sm["v"].val
        fin_used = {ti: V0f + 2 * ti + 2 for ti in range(NT)}
        @block.sync
        def _(sy):
            sy.wait_ge(sm["v"].h, v_bc)
            for ti in range(NT):
                if ti >= 2:
                    sy.wait_ge(sm["v"].h, fin_used[ti - 2])
                sm["d"].inc(sy.dma_start(out=GSB[:, ti % 2, :],
                                         in_=gsil_d.rearrange("(t p) c -> p t c", p=128)[:, ti, :]), 16)
                gfin[ti] = sm["d"].val

        @block.vector
        def _(v):
            for ti in range(NT):
                v.wait_ge(sm["d"].h, gfin[ti])
                if ti == 0:
                    v.wait_ge(sm["v"].h, wo_all)
                sm["v"].inc(v.tensor_mul(scr[:, :], WK[:, ti, :], GSB[:, ti % 2, :]))
                sm["v"].inc(v.tensor_add(X[:, ti, :], scr[:, :], X[:, ti, :]))
                assert fin_used[ti] == sm["v"].val
        fin = sm["v"].val

        @block.sync
        def _(sy):
            sy.wait_ge(sm["v"].h, fin)
            sm["d"].inc(sy.dma_start(out=y.rearrange("(t p) c -> p t c", p=128), in_=X[:, :, :]), 16)
        d_y = sm["d"].val

        @block.sync
        def _(sy):
            sy.wait_ge(sm["d"].h, d_y)

    st.host_prep = _mk_post_prep(st)
    return st


def _mk_post_prep(st):
    def host_prep(b, x_shard, inputs):
        return {
            "x": x_shard,
            "gnw": inputs["L0_attn_gn_weight"][None, :],
            "gnb": inputs["L0_attn_gn_bias"][None, :],
            "woT": np.ascontiguousarray(inputs["L0_attn_W_o"].T),
            "bo_r": rep128(inputs["L0_attn_b_o"]),
            "idn": IDENT128,
            "cvals": st.const_array(),
        }
    return host_prep


_BUILDERS["L0a_pre"] = build_L0a_pre
_BUILDERS["L0a_post"] = build_L0a_post


def run_L0a(x_full, inputs):
    pre = get_stage("L0a_pre")
    shared = pre.host_prep(0, np.ascontiguousarray(x_full[0]), inputs)
    in_maps = [dict(shared, x=np.ascontiguousarray(x_full[b])) for b in range(8)]
    res = run_bass_kernel_spmd(pre.nc, in_maps, core_ids=list(range(8))).results
    khat = [np.asarray(r["khat"], np.float32) for r in res]
    vhat = [np.asarray(r["vhat"], np.float32) for r in res]
    rsig = [np.asarray(r["rsig"], np.float32) for r in res]
    gsil = [np.asarray(r["gsil"], np.float32) for r in res]
    decs = [np.asarray(r["decs"], np.float32).reshape(16) for r in res]
    etas = [np.asarray(r["etas"], np.float32).reshape(16) for r in res]

    scan = get_stage("L0a_scan")
    hsel = np.zeros((128, 2), np.float32)
    hsel[:64, 0] = 1
    hsel[64:, 1] = 1
    state = [np.zeros((128, 512), np.float32) for _ in range(8)]
    wkv = [np.zeros((T, C), np.float32) for _ in range(8)]
    for c in range(T // SCAN_T):
        sl = slice(c * SCAN_T, (c + 1) * SCAN_T)
        in_maps = [{
            "kc": np.ascontiguousarray(khat[b][sl]),
            "vc": np.ascontiguousarray(vhat[b][sl]),
            "rc": np.ascontiguousarray(rsig[b][sl]),
            "s_in": state[b],
            "decs": decs[b], "etas": etas[b],
            "hsel": hsel, "hselT": np.ascontiguousarray(hsel.T),
            "idf": np.eye(128, dtype=np.float32),
            "cvals": scan.const_array(),
        } for b in range(8)]
        res = run_bass_kernel_spmd(scan.nc, in_maps, core_ids=list(range(8))).results
        for b in range(8):
            wkv[b][sl] = np.asarray(res[b]["wkv"], np.float32)
            state[b] = np.ascontiguousarray(np.asarray(res[b]["s_out"], np.float32))

    post = get_stage("L0a_post")
    shared = post.host_prep(0, np.ascontiguousarray(x_full[0]), inputs)
    in_maps = [dict(shared, x=np.ascontiguousarray(x_full[b]),
                    wkv=wkv[b], gsil=gsil[b]) for b in range(8)]
    res = run_bass_kernel_spmd(post.nc, in_maps, core_ids=list(range(8))).results
    return np.stack([np.asarray(r["y"], np.float32) for r in res])
